# version 41
# baseline (speedup 1.0000x reference)
import numpy as np
import ml_dtypes
from contextlib import ExitStack

import concourse.bass as bass
import concourse.mybir as mybir
from concourse.bass_utils import run_bass_kernel_spmd

F32 = mybir.dt.float32
BF16 = mybir.dt.bfloat16
AF = mybir.ActivationFunctionType
ALU = mybir.AluOpType
AX = mybir.AxisListType

ENGINES = ("tensor", "vector", "scalar", "gpsimd", "sync")
COUNTERS = ENGINES + ("cc", "pool", "sdma")
MASKVAL = -100000.0


def full_cfg():
    return dict(T=4096, NH=16, NE=15)



class Op:
    __slots__ = ("eng", "emit", "deps", "dma", "cc", "tok", "need")

    def __init__(self, eng, emit, deps, dma, cc=False):
        self.eng, self.emit, self.deps, self.dma = eng, emit, deps, dma
        self.cc = cc
        self.tok, self.need = None, False


class Buf:
    __slots__ = ("w", "rs")

    def __init__(self):
        self.w = None
        self.rs = {}


class Prog:
    def __init__(self):
        self.segs = [[]]
        self.ops = []

    def new_seg(self):
        self.segs.append([])

    def _deps_for(self, reads, writes, deps):
        dd = [d for d in deps if d is not None]
        for b in reads:
            if b.w is not None:
                dd.append(b.w)
        for b in writes:
            if b.w is not None:
                dd.append(b.w)
            dd.extend(b.rs.values())
        return dd

    def _register(self, op_first, op_last, reads, writes):
        for b in reads:
            b.rs[op_last.eng] = op_last
        for b in writes:
            b.w = op_last
            b.rs = {}

    def add(self, eng, emit, reads=(), writes=(), deps=(), dma=False, cc=False):
        o = Op(eng, emit, self._deps_for(reads, writes, deps), dma, cc)
        self.segs[-1].append(o)
        self.ops.append(o)
        self._register(o, o, reads, writes)
        return o

    def tgroup(self, emits, reads=(), writes=(), deps=()):
        first = Op("tensor", emits[0], self._deps_for(reads, writes, deps), False)
        self.segs[-1].append(first)
        self.ops.append(first)
        last = first
        for e in emits[1:]:
            last = Op("tensor", e, [], False)
            self.segs[-1].append(last)
            self.ops.append(last)
        self._register(first, last, reads, writes)
        return last

    def dma(self, eng, out, in_, reads=(), writes=(), deps=()):
        def emit(e, out=out, in_=in_):
            return e.dma_start(out=out, in_=in_)
        return self.add(eng, emit, reads=reads, writes=writes, deps=deps, dma=True)

    def finalize_and_emit(self, nc, sems):
        for o in self.ops:
            for d in o.deps:
                d.need = True
            if o.dma:
                o.need = True
        for seg in self.segs:
            last = {}
            for o in seg:
                last[o.eng] = o
            for o in last.values():
                o.need = True
        cnt = {e: 0 for e in COUNTERS}
        seg_end = []
        for seg in self.segs:
            for o in seg:
                if o.need:
                    if o.cc:
                        key = "cc"
                    elif o.eng == "gpsimd" and not o.dma:
                        key = "pool"
                    elif o.eng == "scalar" and o.dma:
                        key = "sdma"
                    else:
                        key = o.eng
                    cnt[key] += 16 if o.dma else 1
                    o.tok = (key, cnt[key])
            seg_end.append(dict(cnt))

        self.seg_end = seg_end
        return seg_end[-1]

    def emit_blocks(self, nc, sems):
        seg_end = self.seg_end
        water = {e: {f: 0 for f in COUNTERS} for e in ENGINES}
        for si, seg in enumerate(self.segs):
            eng_ops = {e: [o for o in seg if o.eng == e] for e in ENGINES}
            if not any(eng_ops.values()):
                continue
            with nc.Block(f"seg{si}") as block:
                def mk_body(e):
                    def body(eng, e=e, si=si):
                        if si > 0:
                            prev = seg_end[si - 1]
                            for f in COUNTERS:
                                if f != e and prev[f] > water[e][f]:
                                    eng.wait_ge(sems[f], prev[f])
                                    water[e][f] = prev[f]
                        for o in eng_ops[e]:
                            req = {}
                            if o.dma and o.tok is not None:
                                req[o.tok[0]] = o.tok[1] - 16
                            for d in o.deps:
                                if d.tok is not None:
                                    fe, v = d.tok
                                    if v > req.get(fe, 0):
                                        req[fe] = v
                            for fe, v in req.items():
                                if v > water[e][fe]:
                                    eng.wait_ge(sems[fe], v)
                                    water[e][fe] = v
                            inst = o.emit(eng)
                            if o.tok is not None:
                                inst.then_inc(sems[o.tok[0]], 16 if o.dma else 1)
                    return body
                for e in ENGINES:
                    getattr(block, e)(mk_body(e))



def build_nc(cfg):
    T = cfg["T"]
    NH = cfg["NH"]
    NE = cfg["NE"]
    OWN = T // 2
    HG = NH // 4
    NTC_ALL = T // 128
    NTC_OWN = OWN // 128
    NQT = T // 512
    NQO = OWN // 512
    NHH = NH // 2
    NTT = OWN // 512
    CAP = 512
    SC = CAP // 128

    nc = bass.Bass("TRN2", target_bir_lowering=False, debug=False)

    def din(name, shape, dt=F32):
        return nc.dram_tensor(name, shape, dt, kind="ExternalInput").ap()

    x_all = din("x_all", [T, 1024])
    x_own = din("x_own", [OWN, 1024])
    x_own_fm = din("x_own_fm", [1024, OWN])
    x_kv = din("x_kv", [OWN, 1024])
    wq = din("wq", [1024, 1024])
    wk = din("wk", [1024, 1024])
    wv = din("wv", [1024, 1024])
    wo = din("wo", [1024, 1024])
    rb2 = din("rb2", [NE, 1024])
    gw_pp = din("gw_pp", [128, 8 * NE])
    gb_bc = din("gb_bc", [128, NE])
    bo_pp = din("bo_pp", [128, 8])
    sb2_pp = din("sb2_pp", [128, 8])
    sb1_pp = din("sb1_pp", [128, 32])
    rb1_pp = din("rb1_pp", [128, NE * 16])
    sw1 = din("sw1", [1024, 4096])
    sw2 = din("sw2", [4096, 1024])
    rw1 = din("rw1", [NE, 1024, 2048])
    rw2 = din("rw2", [NE, 2048, 1024])
    masks = din("masks", [128, 2 * 1024], BF16)
    slotiota = din("slotiota", [128, 512], mybir.dt.int16)
    slotid = din("slotid", [128, 4])
    identb = din("identb", [128, 128], BF16)
    identf = din("identf", [128, 128])
    onehf = din("onehf", [NE, NE * 128])
    out_fm = nc.dram_tensor("out_fm", [1024, OWN], F32, kind="ExternalOutput").ap()
    xn_dram = nc.dram_tensor("xn_tm_scratch", [8, 128, OWN], BF16,
                             kind="Internal").ap()
    cc_in = nc.dram_tensor("cc_in", [HG, 2, 4 * 65, OWN], F32).ap()
    cc_out = nc.dram_tensor("cc_out", [HG, 4 * 65, OWN], F32).ap()

    es_global = ExitStack()
    es_ctx = ExitStack()
    es_xnfm = ExitStack()
    es_s1a = ExitStack()
    es_s1b = ExitStack()
    es_s1c = ExitStack()
    es_acc = ExitStack()
    es_s2a = ExitStack()
    es_s2b = ExitStack()
    es_s2w = ExitStack()
    es_s3 = ExitStack()
    es_sem = ExitStack()
    sems = {e: es_sem.enter_context(nc.semaphore(f"ctr_{e}")) for e in COUNTERS}

    def sb(es, name, shape, dt=F32):
        return es.enter_context(nc.sbuf_tensor(name, shape, dt))

    def ps(es, name, shape, dt=F32):
        return es.enter_context(nc.psum_tensor(name, shape, dt))

    pr = Prog()
    B = {}

    def nb(*names):
        for n in names:
            B[n] = Buf()

    identb_sb = sb(es_global, "identb_sb", [128, 128], BF16)
    ones_sb = sb(es_global, "ones_sb", [1, 128], BF16)

    eps_sb = sb(es_global, "eps_sb", [128, 1])
    bo_sb = sb(es_global, "bo_sb", [128, 8])
    sb2_sb = sb(es_global, "sb2_sb", [128, 8])
    sb1_sb = sb(es_global, "sb1_sb", [128, 32])
    rb1_sb = sb(es_global, "rb1_sb", [128, NE * 16])

    nb("masks", "identb", "identf", "ones", "oneh", "eps", "gw", "gb",
       "rb2", "bo", "sb2", "sb1", "rb1", "gfm", "slotiota", "slotid",
       "gated_tm", "pos_tm", "PM", "xndram")

    pr.dma("sync", identb_sb[:, :], identb, writes=[B["identb"]])
    pr.add("vector", lambda v: v.memset(ones_sb[:, :], 1.0), writes=[B["ones"]])
    pr.dma("sync", bo_sb[:, :], bo_pp, writes=[B["bo"]])
    pr.dma("sync", sb2_sb[:, :], sb2_pp, writes=[B["sb2"]])
    pr.dma("sync", sb1_sb[:, :], sb1_pp, writes=[B["sb1"]])
    pr.dma("sync", rb1_sb[:, :], rb1_pp, writes=[B["rb1"]])
    pr.add("vector", lambda v: v.memset(eps_sb[:, :], 1e-5), writes=[B["eps"]])

    ctx_sb = sb(es_ctx, "ctx_sb", [128, 8 * OWN], BF16)
    nb("ctx")
    if NH < 16:
        pr.add("vector", lambda v: v.memset(ctx_sb[:, :], 0.0),
               writes=[B["ctx"]])

    xnfm_sb = sb(es_xnfm, "xnfm_sb", [128, 8 * T], BF16)
    xnkv_sb = sb(es_xnfm, "xnkv_sb", [128, 8 * OWN], BF16)
    nb("xnfm", "xnkv")

    xt_sb = sb(es_s1a, "xt_sb", [128, 4 * 1024])
    xnt_sb = sb(es_s1a, "xnt_sb", [128, 4 * 1024], BF16)
    stat_sb = sb(es_s1a, "stat_sb", [128, 4 * 12])
    mv_sb = sb(es_s1a, "mv_sb", [128, 4 * 2])
    rstd_sb = sb(es_s1a, "rstd_sb", [128, 4])
    negmr_sb = sb(es_s1a, "negmr_sb", [128, 4])
    psE = [ps(es_s1a, f"psE{i}", [128, 1024], BF16) for i in range(4)]

    nb("xt0", "xt1", "xt2", "xt3", "xnt0", "xnt1", "xnt2", "xnt3",
       "stat0", "stat1", "stat2", "stat3", "mv0", "mv1", "mv2", "mv3",
       "rstd0", "rstd1", "rstd2", "rstd3",
       "negmr0", "negmr1", "negmr2", "negmr3",
       "pE0", "pE1", "pE2", "pE3")

    def emit_ln_a(tc, x_tiled, pfx, xts_t, stat_t, mv_t, rstd_t, negmr_t,
                  depth=4):
        par = tc % depth
        xtB = B[f"{pfx}xt{par}"]
        stB, mvB = B[f"{pfx}stat{par}"], B[f"{pfx}mv{par}"]
        rsB, nmB = B[f"{pfx}rstd{par}"], B[f"{pfx}negmr{par}"]
        xts = xts_t[:, par * 1024:(par + 1) * 1024]
        pr.dma("sync" if tc % 2 == 0 else "scalar", xts, x_tiled[tc],
               writes=[xtB])
        st = stat_t[:, par * 12:(par + 1) * 12]
        pr.add("vector", lambda v, xts=xts, st=st: v.bn_stats(st[:, 0:6], xts[:, 0:512]),
               reads=[xtB], writes=[stB])
        pr.add("vector", lambda v, xts=xts, st=st: v.bn_stats(st[:, 6:12], xts[:, 512:1024]),
               reads=[xtB], writes=[stB])
        mv = mv_t[:, par * 2:(par + 1) * 2]
        pr.add("vector", lambda v, st=st, mv=mv: v.bn_aggr(mv, st),
               reads=[stB], writes=[mvB])
        rs = rstd_t[:, par:par + 1]
        pr.add("scalar", lambda s, mv=mv, rs=rs: s.activation(
            rs, mv[:, 1:2], AF.Ln, bias=eps_sb[:, 0:1]),
            reads=[mvB, B["eps"]], writes=[rsB])
        pr.add("scalar", lambda s, rs=rs: s.activation(rs, rs, AF.Exp, scale=-0.5),
               reads=[rsB], writes=[rsB])

    def emit_ln_b(tc, pfx, xts_t, xnts_t, rstd_t, negmr_t, depth=4,
                  mv_t=None, stat_t=None):
        par = tc % depth
        xtB, xntB = B[f"{pfx}xt{par}"], B[f"{pfx}xnt{par}"]
        rsB, nmB = B[f"{pfx}rstd{par}"], B[f"{pfx}negmr{par}"]
        mvB = B[f"{pfx}mv{par}"]
        xts = xts_t[:, par * 1024:(par + 1) * 1024]
        rs = rstd_t[:, par:par + 1]
        nm = negmr_t[:, par:par + 1]
        mv = mv_t[:, par * 2:(par + 1) * 2]
        pr.add("vector", lambda v, mv=mv, rs=rs, nm=nm: v.tensor_scalar(
            nm, mv[:, 0:1], rs, -1.0, op0=ALU.mult, op1=ALU.mult),
            reads=[mvB, rsB], writes=[nmB])
        xnts = xnts_t[:, par * 1024:(par + 1) * 1024]
        pr.add("scalar", lambda s, xts=xts, xnts=xnts, rs=rs, nm=nm: s.activation(
            xnts, xts, AF.Identity, scale=rs, bias=nm),
            reads=[xtB, rsB, nmB], writes=[xntB])
        return xnts, xntB

    x_all_t = x_all.rearrange("(tc p) d -> tc p d", p=128)
    x_kv_t = x_kv.rearrange("(tc p) d -> tc p d", p=128)
    for src_t, n_t, dst_t, dstB, W in ((x_all_t, NTC_ALL, xnfm_sb, B["xnfm"], T),
                                       (x_kv_t, NTC_OWN, xnkv_sb, B["xnkv"], OWN)):
        for it in range(n_t + 2):
            if it >= 2:
                tc = it - 2
                xnts, xntB = emit_ln_b(tc, "", xt_sb, xnt_sb, rstd_sb, negmr_sb,
                                       mv_t=mv_sb)
            if it < n_t:
                emit_ln_a(it, src_t, "", xt_sb, stat_sb, mv_sb, rstd_sb, negmr_sb)
            if it < 2:
                continue
            for j in range(8):
                pb = B[f"pE{j % 4}"]
                pt = psE[j % 4][:, 0:128]
                pr.add("tensor", lambda t, pt=pt, src=xnts[:, j * 128:(j + 1) * 128]:
                       t.transpose(pt, src, identb_sb[:, :]),
                       reads=[xntB, B["identb"]], writes=[pb])
                dst = dst_t[:, j * W + tc * 128: j * W + tc * 128 + 128]
                pr.add("vector", lambda v, pt=pt, dst=dst: v.tensor_copy(dst, pt),
                       reads=[pb], writes=[dstB])

    es_s1a.close()

    pr.new_seg()

    KT = OWN
    NKC = KT // 128

    masks_sb = sb(es_s1b, "masks_sb", [128, 2 * 1024], BF16)
    q_sb = sb(es_s1b, "q_sb", [128, 2 * T], BF16)
    k_sb = sb(es_s1b, "k_sb", [128, 2 * KT], BF16)
    v_sb = sb(es_s1b, "v_sb", [128, NKC * 260], BF16)
    pt_sb = sb(es_s1b, "pt_sb", [128, 5 * 1024], BF16)
    wq_sb = sb(es_s1b, "wq_sb", [128, 8 * 256], BF16)
    wk_sb = sb(es_s1b, "wk_sb", [128, 8 * 256], BF16)
    wv_sb = sb(es_s1b, "wv_sb", [128, 8 * 256], BF16)
    stg_sb = sb(es_s1b, "stg_sb", [128, 2 * 1024])

    psA = ps(es_s1b, "psA", [128, 1024])
    psB = ps(es_s1b, "psB", [128, 1024])
    psG = ps(es_s1b, "psG", [128, 1024])
    psC = ps(es_s1b, "psC", [128, 512])
    psD = ps(es_s1b, "psD", [128, 512])

    pr.dma("sync", masks_sb[:, :], masks, writes=[B["masks"]])

    for k_ in range(HG):
        nb(f"ccin{k_}", f"ccout{k_}")
    nb("q", "k", "v", "wq", "wk", "wv", "stg0", "stg1",
       "pt0", "pt1", "pt2", "pt3", "pt4",
       "pA0", "pA1", "pB0", "pB1", "pG", "pC", "pD")

    wq_r = wq.rearrange("(kc p) o -> p kc o", p=128)
    wk_r = wk.rearrange("(kc p) o -> p kc o", p=128)
    wv_r = wv.rearrange("(kc p) o -> p kc o", p=128)

    it_hp = [0]
    gbase = [0]
    pend_tail = [None]

    def flush_tail():
        if pend_tail[0] is not None:
            fn, pend_tail[0] = pend_tail[0], None
            fn()

    def emit_cc(k):
        pr.add("gpsimd", lambda g, k=k: g.collective_compute(
            "ReduceScatter", ALU.add,
            replica_groups=[[2 * i, 2 * i + 1] for i in range(4)],
            ins=[cc_in[k:k + 1].rearrange("a q r t -> (a q r) t").opt()],
            outs=[cc_out[k:k + 1].rearrange("a r t -> (a r) t").opt()]),
            reads=[B[f"ccin{k}"]], writes=[B[f"ccout{k}"]], cc=True)

    for hg in range(HG):
        flush_tail()
        for wsb, wr, bn in ((wq_sb, wq_r, "wq"), (wk_sb, wk_r, "wk"),
                            (wv_sb, wv_r, "wv")):
            pr.dma("gpsimd",
                   wsb[:, :].rearrange("p (kc o) -> p kc o", kc=8),
                   wr[:, :, hg * 256:(hg + 1) * 256],
                   writes=[B[bn]])
        if hg >= 1:
            emit_cc(hg - 1)
        for dst_sb, dstB, wsb, wB, src_sb, srcB, W, pst, pb0, pb1 in (
                (q_sb, "q", wq_sb, "wq", xnfm_sb, B["xnfm"], T, psA, "pA0", "pA1"),
                (k_sb, "k", wk_sb, "wk", xnkv_sb, B["xnkv"], KT, psB, "pB0", "pB1")):
            for hp in range(2):
                for qt in range(W // 512):
                    i = qt % 2
                    pv = pst[:, i * 512: i * 512 + 512]
                    pvB = B[pb0] if i == 0 else B[pb1]
                    emits = []
                    for kc in range(8):
                        emits.append(
                            lambda t, pv=pv,
                            l=wsb[:, kc * 256 + hp * 128: kc * 256 + hp * 128 + 128],
                            r=src_sb[:, kc * W + qt * 512: kc * W + qt * 512 + 512],
                            s=(kc == 0), e=(kc == 7):
                            t.matmul(pv, l, r, start=s, stop=e))
                    pr.tgroup(emits, reads=[B[wB], srcB], writes=[pvB])
                    dst = dst_sb[:, hp * W + qt * 512: hp * W + qt * 512 + 512]
                    pr.add("vector", lambda v, pv=pv, dst=dst: v.tensor_copy(dst, pv),
                           reads=[pvB], writes=[B[dstB]])
        pr.add("vector", lambda v: v.memset(
            v_sb[:, :].rearrange("p (c h d) -> p c h d", c=NKC, h=4)[:, :, :, 64:65],
            1.0), writes=[B["v"]])
        for tkc in range(NKC):
            i = tkc % 2
            pv = (psC if i == 0 else psD)[:, 0:256]
            pvB = B["pC"] if i == 0 else B["pD"]
            emits = []
            for kc in range(8):
                emits.append(
                    lambda t, pv=pv,
                    l=xnkv_sb[:, kc * KT + tkc * 128: kc * KT + tkc * 128 + 128],
                    r=wv_sb[:, kc * 256: kc * 256 + 256],
                    s=(kc == 0), e=(kc == 7):
                    t.matmul(pv, l, r, start=s, stop=e))
            pr.tgroup(emits, reads=[B["wv"], B["xnkv"]], writes=[pvB])
            dst = v_sb[:, tkc * 260: tkc * 260 + 260].rearrange(
                "p (h d) -> p h d", d=65)[:, :, 0:64]
            src = pv.rearrange("p (h d) -> p h d", d=64)
            pr.add("vector", lambda v, src=src, dst=dst: v.tensor_copy(dst, src),
                   reads=[pvB], writes=[B["v"]])

        for hp in range(2):
            for p in range(NQT):
                par = it_hp[0] % 2
                it_hp[0] += 1
                ncl = 2 * p + 2
                base = gbase[0]
                gbase[0] += ncl
                psS_ring = (psA, psB, psG)
                sB_ring = ([B["pA0"], B["pA1"]], [B["pB0"], B["pB1"]], [B["pG"]])

                def emit_av(ca, cb, ncl=ncl, hp=hp, base=base):
                    for half, avp, avB in ((0, psC, B["pC"]), (1, psD, B["pD"])):
                        emits = []
                        for c in (ca, cb):
                            sl = (base + c) % 5
                            ptv = pt_sb[:, sl * 1024 + half * 512:
                                        sl * 1024 + half * 512 + 512]
                            emits.append(
                                lambda t, avp=avp,
                                l=v_sb[:, c * 260 + (2 * hp + half) * 65:
                                       c * 260 + (2 * hp + half) * 65 + 65],
                                r=ptv, st=(c == 0), e=(c == ncl - 1):
                                t.matmul(avp[0:65, :], l, r, start=st, stop=e))
                        pr.tgroup(emits,
                                  reads=[B["v"], B[f"pt{(base + ca) % 5}"],
                                         B[f"pt{(base + cb) % 5}"]],
                                  writes=[avB])

                for b in range(ncl):
                    psS = psS_ring[b % 3]
                    sB = sB_ring[b % 3]
                    emits = []
                    for half in (0, 1):
                        emits.append(
                            lambda t, pv=psS[:, half * 512:(half + 1) * 512],
                            l=k_sb[64 * half:64 * half + 64,
                                   hp * KT + b * 128: hp * KT + b * 128 + 128],
                            r=q_sb[64 * half:64 * half + 64,
                                   hp * T + p * 512: hp * T + p * 512 + 512]:
                            t.matmul(pv, l, r, start=True, stop=True))
                    pr.tgroup(emits, reads=[B["k"], B["q"]], writes=sB)
                    sl = (base + b) % 5
                    ptB = B[f"pt{sl}"]
                    ptv = pt_sb[:, sl * 1024:sl * 1024 + 1024]
                    pr.add("scalar", lambda sc, pv=psS[:, 0:1024], ptv=ptv:
                           sc.activation(ptv, pv, AF.Exp, scale=0.125),
                           reads=sB, writes=[ptB])
                    if b >= 2 * p:
                        i01 = b - 2 * p
                        pr.add("vector", lambda v, ptv=ptv,
                               m=masks_sb[:, i01 * 1024: i01 * 1024 + 1024]:
                               v.tensor_tensor(ptv, ptv, m, op=ALU.mult),
                               reads=[B["masks"], ptB], writes=[ptB])
                    if b == 1:
                        flush_tail()
                    if b >= 3 and b % 2 == 1:
                        emit_av(b - 3, b - 2)

                def tail(emit_av=emit_av, ncl=ncl, par=par, hg=hg, hp=hp, p=p):
                    emit_av(ncl - 2, ncl - 1)
                    stg = stg_sb[:, par * 1024:(par + 1) * 1024]
                    stgB = B[f"stg{par}"]
                    pr.add("vector", lambda v, stg=stg: v.tensor_copy(
                        stg[0:65, 0:512], psC[0:65, :]),
                        reads=[B["pC"]], writes=[stgB])
                    pr.add("vector", lambda v, stg=stg: v.tensor_copy(
                        stg[0:65, 512:1024], psD[0:65, :]),
                        reads=[B["pD"]], writes=[stgB])
                    h_even = hg * 4 + 2 * hp
                    qh, pl = divmod(p, NQT // 2)
                    for half in (0, 1):
                        h_ = h_even + half
                        hh_, hl_ = divmod(h_, 4)
                        pr.dma("sync",
                               cc_in[hh_, qh, hl_ * 65:(hl_ + 1) * 65,
                                     pl * 512: pl * 512 + 512],
                               stg[0:65, half * 512: half * 512 + 512],
                               reads=[stgB], writes=[B[f"ccin{hh_}"]])
                pend_tail[0] = tail

    flush_tail()
    es_s1b.close()

    es_xnfm.close()

    pr.new_seg()

    acc_sb = sb(es_acc, "acc_sb", [128, 8 * OWN])
    gated_tm_sb = sb(es_acc, "gated_tm", [128, NTC_OWN * NE], BF16)
    pos_tm_sb = sb(es_acc, "pos_tm", [128, NTC_OWN * NE])
    pm_sb = sb(es_acc, "pm_sb", [NE, OWN])
    if 8 * OWN >= 16384:
        pool_sb = ctx_sb
    else:
        pool_sb = sb(es_acc, "pool_sb", [128, 16384], BF16)

    xnofm_sb = sb(es_s2a, "xnofm_sb", [128, 8 * OWN], BF16)

    wo_sb = sb(es_s2b, "wo_sb", [128, 8 * 1024], BF16)
    xt2_sb = sb(es_s2b, "xt2_sb", [128, 2 * 1024])
    xnt2_sb = sb(es_s2b, "xnt2_sb", [128, 2 * 1024])
    xnbf_sb = sb(es_s2b, "xnbf_sb", [128, 2 * 1024], BF16)
    stat2_sb = sb(es_s2b, "stat2_sb", [128, 2 * 12])
    mv2_sb = sb(es_s2b, "mv2_sb", [128, 2 * 2])
    rstd2_sb = sb(es_s2b, "rstd2_sb", [128, 2])
    negmr2_sb = sb(es_s2b, "negmr2_sb", [128, 2])
    gwork_sb = sb(es_s2b, "gwork_sb", [128, 2 * 7 * 16])
    gsc_sb = sb(es_s2b, "gsc_sb", [128, 2 * 16])
    gpad_sb = sb(es_s2b, "gpad_sb", [128, 2 * 128], BF16)
    gfm_sb = sb(es_s2b, "gfm_sb", [NE, OWN], BF16)
    mfm_sb = sb(es_s2b, "mfm_sb", [NE, OWN], BF16)
    identf_sb = sb(es_s2b, "identf_sb", [128, 128])
    rstg_sb = sb(es_s2b, "rstg_sb", [64, OWN])
    lpf_sb = sb(es_s2b, "lpf_sb", [4, OWN // 2])
    lpb_sb = sb(es_s2b, "lpb_sb", [4, OWN], BF16)
    lr_sb = sb(es_s2b, "lr_sb", [128, OWN], BF16)
    onesb_sb = sb(es_s2b, "onesb_sb", [128, 64], BF16)
    gw_sb = sb(es_s2b, "gw_sb", [128, 8 * NE], BF16)
    gb_sb = sb(es_s2b, "gb_sb", [128, NE])
    rb2_sb = sb(es_s2b, "rb2_sb", [NE, 1024], BF16)

    psA2 = ps(es_s2b, "psA2", [128, 1024])
    psB2 = ps(es_s2b, "psB2", [128, 1024])
    psC2 = ps(es_s2b, "psC2", [128, 512])
    psD2 = ps(es_s2b, "psD2", [128, 512])
    psG2 = ps(es_s2b, "psG2", [128, 1024], BF16)
    psH2 = ps(es_s2b, "psH2", [128, 512])

    nb("acc", "xnofm", "wo", "2xt0", "2xt1", "2xt2", "2xt3",
       "2xnt0", "2xnt1", "2xnt2", "2xnt3",
       "xnbf0", "xnbf1", "xnbf2", "xnbf3",
       "2stat0", "2stat1", "2stat2", "2stat3", "2mv0", "2mv1", "2mv2", "2mv3",
       "2rstd0", "2rstd1", "2rstd2", "2rstd3",
       "2negmr0", "2negmr1", "2negmr2", "2negmr3",
       "gwork0", "gwork1", "gsc0", "gsc1", "gpad0", "gpad1", "mfm",
       "rstg", "lpf", "lr", "onesb",
       "lpb", "p2A0", "p2A1", "p2B0", "p2B1", "p2C", "p2D", "p2E", "p2F")

    pr.dma("sync", identf_sb[:, :], identf, writes=[B["identf"]])
    pr.dma("sync", gb_sb[:, :], gb_bc, writes=[B["gb"]])
    pr.dma("gpsimd", gw_sb[:, :], gw_pp, writes=[B["gw"]])

    x_own_fm_r = x_own_fm.rearrange("(oc p) t -> p oc t", p=128)
    acc_r = acc_sb[:, :].rearrange("p (oc t) -> p oc t", oc=8)
    pr.dma("scalar", acc_r[:, 0:4], x_own_fm_r[:, 0:4], writes=[B["acc"]])
    pr.dma("gpsimd", acc_r[:, 4:8], x_own_fm_r[:, 4:8], writes=[B["acc"]])

    wo_r = wo.rearrange("(kc p) o -> p kc o", p=128)
    pr.dma("gpsimd", wo_sb[:, :].rearrange("p (kc o) -> p kc o", kc=8), wo_r,
           writes=[B["wo"]])
    pr.dma("gpsimd", rb2_sb[:, :], rb2, writes=[B["rb2"]])

    emit_cc(HG - 1)

    pr.add("vector", lambda v: v.memset(onesb_sb[:, :], 1.0),
           writes=[B["onesb"]])

    def emit_lq(k):
        rows = cc_out[k:k + 1].rearrange("a (hl r) t -> (a hl) r t", r=65)
        half = OWN // 2
        for ch in range(2):
            pr.dma("sync", lpf_sb[:, :], rows[:, 64, ch * half:(ch + 1) * half],
                   reads=[B[f"ccout{k}"]], writes=[B["lpf"]])
            pr.add("scalar", lambda s: s.activation(lpf_sb[:, :], lpf_sb[:, :],
                   AF.Ln), reads=[B["lpf"]], writes=[B["lpf"]])
            pr.add("scalar", lambda s, ch=ch: s.activation(
                lpb_sb[:, ch * half:(ch + 1) * half], lpf_sb[:, :],
                AF.Exp, scale=-1.0), reads=[B["lpf"]], writes=[B["lpb"]])

    def emit_rhead(h):
            hh, hl = divmod(h, 4)
            pr.dma("sync", rstg_sb[:, :],
                   cc_out[hh, hl * 65:hl * 65 + 64, :],
                   reads=[B[f"ccout{hh}"]], writes=[B["rstg"]])
            pr.dma("sync", lr_sb[64:65, :], lpb_sb[hl:hl + 1, :],
                   reads=[B["lpb"]], writes=[B["lr"]])

    def emit_readback(h, qt):
            pbase = 64 * (h % 2)
            h2 = h // 2
            par = (h * NQO + qt) % 2
            bc = psA2[:, par * 512:(par + 1) * 512]
            bcB = B["p2A0"] if par == 0 else B["p2A1"]
            pr.add("tensor", lambda t, bc=bc, qt=qt: t.matmul(
                bc[0:64, :], onesb_sb[64:65, :],
                lr_sb[64:65, qt * 512: qt * 512 + 512],
                start=True, stop=True), reads=[B["onesb"], B["lr"]],
                writes=[bcB])
            dst = ctx_sb[pbase:pbase + 64, h2 * OWN + qt * 512:
                         h2 * OWN + qt * 512 + 512]
            pr.add("vector", lambda v, dst=dst, qt=qt, bc=bc:
                v.tensor_tensor(dst, rstg_sb[0:64, qt * 512: qt * 512 + 512],
                                bc[0:64, :], op=ALU.mult),
                reads=[B["rstg"], bcB], writes=[B["ctx"]])

    rb_work = [(h, qt) for h in range(NH) for qt in range(NQO)]
    rb_pos = [0]
    lq_done = [False] * HG

    def drain_readback(n):
        while rb_pos[0] < len(rb_work) and n > 0:
            h, qt = rb_work[rb_pos[0]]
            if not lq_done[h // 4]:
                emit_lq(h // 4)
                lq_done[h // 4] = True
            if qt == 0:
                emit_rhead(h)
            emit_readback(h, qt)
            rb_pos[0] += 1
            n -= 1

    x_own_t = x_own.rearrange("(tc p) d -> tc p d", p=128)

    def emit_gate(tc):
        gp = tc % 2
        i = tc % 2
        pg = (psC2 if i == 0 else psD2)[:, 0:NE]
        pgB = B["p2C"] if i == 0 else B["p2D"]
        emits = []
        for kc in range(8):
            emits.append(
                lambda t, pg=pg,
                l=xnofm_sb[:, kc * OWN + tc * 128: kc * OWN + tc * 128 + 128],
                r=gw_sb[:, kc * NE: kc * NE + NE],
                s=(kc == 0), e=(kc == 7):
                t.matmul(pg, l, r, start=s, stop=e))
        pr.tgroup(emits, reads=[B["xnofm"], B["gw"]], writes=[pgB])
        W = gwork_sb[:, gp * 112: gp * 112 + 112]
        gx = W[:, 0:NE]; ge = W[:, 16:16 + NE]; g = W[:, 32:32 + NE]
        g1 = W[:, 48:48 + NE]; g2 = W[:, 64:64 + NE]; tmp = W[:, 80:80 + NE]
        gated = W[:, 96:96 + NE]
        S = gsc_sb[:, gp * 16: gp * 16 + 16]
        mx = S[:, 0:1]; nmx = S[:, 1:2]; sm = S[:, 2:3]; rsm = S[:, 3:4]
        m1 = S[:, 4:5]; m2 = S[:, 5:6]; m3 = S[:, 6:7]
        gwB, gsB = B[f"gwork{gp}"], B[f"gsc{gp}"]
        gpad = gpad_sb[:, gp * 128: gp * 128 + 128]
        gpB = B[f"gpad{gp}"]

        def va(fn, reads, writes):
            pr.add("gpsimd", fn, reads=reads, writes=writes)

        pr.add("vector", lambda v, pg=pg, gx=gx: v.tensor_tensor(
            gx, pg, gb_sb[:, :], op=ALU.add), reads=[pgB, B["gb"]], writes=[gwB])
        pr.add("vector", lambda v, gx=gx, mx=mx: v.reduce_max(mx, gx, axis=AX.X), reads=[gwB], writes=[gsB])
        va(lambda v, mx=mx, nmx=nmx: v.tensor_scalar_mul(nmx, mx, -1.0), [gsB], [gsB])
        pr.add("scalar", lambda s, gx=gx, ge=ge, nmx=nmx: s.activation(
            ge, gx, AF.Exp, bias=nmx), reads=[gwB, gsB], writes=[gwB])
        pr.add("vector", lambda v, ge=ge, sm=sm: v.reduce_sum(sm, ge, axis=AX.X), reads=[gwB], writes=[gsB])
        pr.add("vector", lambda v, sm=sm, rsm=rsm: v.reciprocal(rsm, sm),
               reads=[gsB], writes=[gsB])
        va(lambda v, ge=ge, g=g, rsm=rsm: v.tensor_scalar_mul(g, ge, rsm),
           [gwB, gsB], [gwB])
        pr.add("vector", lambda v, g=g, m1=m1: v.reduce_max(m1, g, axis=AX.X), reads=[gwB], writes=[gsB])
        va(lambda v, g=g, m1=m1, tmp=tmp: v.tensor_scalar(
            tmp, g, m1, None, op0=ALU.is_ge), [gwB, gsB], [gwB])
        va(lambda v, g=g, tmp=tmp: v.tensor_tensor(tmp, tmp, g, op=ALU.mult),
           [gwB], [gwB])
        va(lambda v, g=g, g1=g1, tmp=tmp: v.tensor_tensor(g1, g, tmp, op=ALU.subtract),
           [gwB], [gwB])
        pr.add("vector", lambda v, g1=g1, m2=m2: v.reduce_max(m2, g1, axis=AX.X), reads=[gwB], writes=[gsB])
        va(lambda v, g1=g1, m2=m2, tmp=tmp: v.tensor_scalar(
            tmp, g1, m2, None, op0=ALU.is_ge), [gwB, gsB], [gwB])
        va(lambda v, g1=g1, tmp=tmp: v.tensor_tensor(tmp, tmp, g1, op=ALU.mult),
           [gwB], [gwB])
        va(lambda v, g1=g1, g2=g2, tmp=tmp: v.tensor_tensor(g2, g1, tmp, op=ALU.subtract),
           [gwB], [gwB])
        pr.add("vector", lambda v, g2=g2, m3=m3: v.reduce_max(m3, g2, axis=AX.X), reads=[gwB], writes=[gsB])
        va(lambda v, g=g, m3=m3, tmp=tmp: v.tensor_scalar(
            tmp, g, m3, None, op0=ALU.is_ge), [gwB, gsB], [gwB])
        va(lambda v, g=g, tmp=tmp, gated=gated: v.tensor_tensor(
            gated, g, tmp, op=ALU.mult), [gwB], [gwB])
        va(lambda v, gated=gated, tc=tc: v.tensor_copy(
            gated_tm_sb[:, tc * NE: tc * NE + NE], gated), [gwB], [B["gated_tm"]])
        va(lambda v, gated=gated, gpad=gpad: v.tensor_copy(gpad[:, 0:NE], gated),
           [gwB], [gpB])
        pr.add("tensor", lambda t, gpad=gpad: t.transpose(
            psG2[:, 0:128], gpad, identb_sb[:, :]),
            reads=[gpB, B["identb"]], writes=[B["p2E"]])
        pr.add("vector", lambda v, tc=tc: v.tensor_copy(
            gfm_sb[0:NE, tc * 128:(tc + 1) * 128], psG2[0:NE, 0:128]),
            reads=[B["p2E"]], writes=[B["gfm"]])

    pr.add("vector", lambda v: v.memset(gpad_sb[:, :], 0.0),
           writes=[B["gpad0"], B["gpad1"]])
    for it in range(NTC_OWN + 1):
        if it >= 1:
            tc = it - 1
            par = tc % 2
            xnts, xntB = emit_ln_b(tc, "2", xt2_sb, xnt2_sb, rstd2_sb, negmr2_sb,
                                   depth=2, mv_t=mv2_sb)
        if it < NTC_OWN:
            emit_ln_a(it, x_own_t, "2", xt2_sb, stat2_sb, mv2_sb, rstd2_sb,
                      negmr2_sb, depth=2)
        if it < 1:
            continue
        for j in range(8):
            pb = B["p2F"] if j % 2 == 0 else B["p2D"]
            pt = (psH2 if j % 2 == 0 else psD2)[:, 0:128]
            pr.add("tensor", lambda t, pt=pt, src=xnts[:, j * 128:(j + 1) * 128]:
                   t.transpose(pt, src, identf_sb[:, :]),
                   reads=[xntB, B["identf"]], writes=[pb])
            dst = acc_sb[:, j * OWN + tc * 128: j * OWN + tc * 128 + 128]
            pr.add("vector", lambda v, pt=pt, dst=dst: v.tensor_tensor(
                dst, dst, pt, op=ALU.add), reads=[pb, B["acc"]], writes=[B["acc"]])
            dst2 = xnofm_sb[:, j * OWN + tc * 128: j * OWN + tc * 128 + 128]
            pr.add("vector", lambda v, pt=pt, dst2=dst2: v.tensor_copy(dst2, pt),
                   reads=[pb], writes=[B["xnofm"]])
        xnbf = xnbf_sb[:, par * 1024:(par + 1) * 1024]
        xnbB = B[f"xnbf{par}"]
        pr.add("gpsimd", lambda g, xnts=xnts, xnbf=xnbf: g.tensor_copy(xnbf, xnts),
               reads=[xntB], writes=[xnbB])
        pr.dma("sync", xn_dram[:, :, tc * 128:(tc + 1) * 128].rearrange(
                   "dc p q -> p dc q"),
               xnbf.rearrange("p (dc q) -> p dc q", dc=8),
               reads=[xnbB], writes=[B["xndram"]])
        emit_gate(tc)
        drain_readback(4)

    drain_readback(len(rb_work))

    for oc in range(8):
        pr.add("vector", lambda v, oc=oc: v.tensor_scalar(
            acc_sb[:, oc * OWN:(oc + 1) * OWN], acc_sb[:, oc * OWN:(oc + 1) * OWN],
            bo_sb[:, oc:oc + 1], sb2_sb[:, oc:oc + 1], op0=ALU.add, op1=ALU.add),
            reads=[B["acc"], B["bo"], B["sb2"]], writes=[B["acc"]])


    pr.add("vector", lambda v: v.tensor_scalar(
        mfm_sb[:, :], gfm_sb[0:NE, :], 0.0, None, op0=ALU.is_gt),
        reads=[B["gfm"]], writes=[B["mfm"]])
    pr.add("vector", lambda v: v.tensor_tensor_scan(
        pm_sb[:, :], mfm_sb[:, :], mfm_sb[:, :], 0.0,
        op0=ALU.add, op1=ALU.bypass), reads=[B["mfm"]], writes=[B["PM"]])
    pr.add("vector", lambda v: v.tensor_tensor(
        pm_sb[:, :], pm_sb[:, :], mfm_sb[:, :], op=ALU.mult),
        reads=[B["PM"], B["mfm"]], writes=[B["PM"]])
    pr.add("vector", lambda v: v.tensor_scalar_add(pm_sb[:, :], pm_sb[:, :], -1.0),
           reads=[B["PM"]], writes=[B["PM"]])
    for tc in range(NTC_OWN):
        pr.add("tensor", lambda t, tc=tc: t.transpose(
            psH2[:, 0:NE], pm_sb[0:NE, tc * 128:(tc + 1) * 128],
            identf_sb[0:NE, 0:NE]),
            reads=[B["PM"], B["identf"]], writes=[B["p2F"]])
        pr.add("vector", lambda v, tc=tc: v.tensor_copy(
            pos_tm_sb[:, tc * NE: tc * NE + NE], psH2[:, 0:NE]),
            reads=[B["p2F"]], writes=[B["pos_tm"]])

    for oc in range(8):
        for qt in range(NTT):
            i = (oc * NTT + qt) % 2
            pv = psA2[:, i * 512: i * 512 + 512]
            pvB = B["p2A0"] if i == 0 else B["p2A1"]
            emits = []
            for kc in range(8):
                emits.append(
                    lambda t, pv=pv,
                    l=wo_sb[:, kc * 1024 + oc * 128: kc * 1024 + oc * 128 + 128],
                    r=ctx_sb[:, kc * OWN + qt * 512: kc * OWN + qt * 512 + 512],
                    s=(kc == 0), e=(kc == 7):
                    t.matmul(pv, l, r, start=s, stop=e))
            pr.tgroup(emits, reads=[B["wo"], B["ctx"]], writes=[pvB])
            dst = acc_sb[:, oc * OWN + qt * 512: oc * OWN + qt * 512 + 512]
            pr.add("vector", lambda v, pv=pv, dst=dst: v.scalar_tensor_tensor(
                dst, pv, 1.0, dst, op0=ALU.mult, op1=ALU.add),
                reads=[pvB, B["acc"]], writes=[B["acc"]])

    for oc in range(8):
        for qt in range(NTT):
            i = (oc * NTT + qt) % 2
            pv = psB2[:, i * 512: i * 512 + 512]
            pvB = B["p2B0"] if i == 0 else B["p2B1"]
            pr.add("tensor", lambda t, pv=pv,
                   l=rb2_sb[0:NE, oc * 128:(oc + 1) * 128],
                   r=gfm_sb[0:NE, qt * 512: qt * 512 + 512]:
                   t.matmul(pv, l, r, start=True, stop=True),
                   reads=[B["rb2"], B["gfm"]], writes=[pvB])
            dst = acc_sb[:, oc * OWN + qt * 512: oc * OWN + qt * 512 + 512]
            pr.add("vector", lambda v, pv=pv, dst=dst: v.scalar_tensor_tensor(
                dst, pv, 1.0, dst, op0=ALU.mult, op1=ALU.add),
                reads=[pvB, B["acc"]], writes=[B["acc"]])

    es_s2b.close()

    pr.new_seg()

    w1s_sb = sb(es_s2w, "w1s_sb", [128, 8 * 2048], BF16)
    w2s_sb = sb(es_s2w, "w2s_sb", [128, 16 * 1024], BF16)
    psA4 = ps(es_s2w, "psA4", [128, 1024])
    psB4 = ps(es_s2w, "psB4", [128, 1024])
    nb("w1s", "w2s", "p4A0", "p4A1", "p4B0", "p4B1", "hS")
    h_sb = pool_sb

    sw1_r = sw1.rearrange("(kc p) f -> p kc f", p=128)
    sw2_r = sw2.rearrange("(fc p) o -> p fc o", p=128)

    for half in range(2):
        src1 = sw1_r[:, :, half * 2048:(half + 1) * 2048]
        src2 = sw2_r[:, half * 16:(half + 1) * 16, :]
        pr.dma("gpsimd", w1s_sb[:, :].rearrange("p (kc f) -> p kc f", kc=8), src1,
               writes=[B["w1s"]])
        pr.dma("gpsimd", w2s_sb[:, :].rearrange("p (fc o) -> p fc o", fc=16), src2,
               writes=[B["w2s"]])
        for tt in range(NTT):
            for fc in range(16):
                i = fc % 2
                ph = psA4[:, i * 512: i * 512 + 512]
                phB = B["p4A0"] if i == 0 else B["p4A1"]
                emits = []
                for kc in range(8):
                    emits.append(
                        lambda t, ph=ph,
                        l=w1s_sb[:, kc * 2048 + fc * 128: kc * 2048 + fc * 128 + 128],
                        r=xnofm_sb[:, kc * OWN + tt * 512: kc * OWN + tt * 512 + 512],
                        s=(kc == 0), ee=(kc == 7):
                        t.matmul(ph, l, r, start=s, stop=ee))
                pr.tgroup(emits, reads=[B["w1s"], B["xnofm"]], writes=[phB])
                b1col = sb1_sb[:, half * 16 + fc: half * 16 + fc + 1]
                hdst = h_sb[:, fc * 512:(fc + 1) * 512]
                pr.add("scalar", lambda s, ph=ph, hdst=hdst, b1col=b1col:
                       s.activation(hdst, ph, AF.Gelu, bias=b1col),
                       reads=[phB, B["sb1"]], writes=[B["hS"]])
            for oc in range(8):
                i = oc % 2
                pv = psB4[:, i * 512: i * 512 + 512]
                pvB = B["p4B0"] if i == 0 else B["p4B1"]
                emits = []
                for fc in range(16):
                    emits.append(
                        lambda t, pv=pv,
                        l=w2s_sb[:, fc * 1024 + oc * 128: fc * 1024 + oc * 128 + 128],
                        r=h_sb[:, fc * 512:(fc + 1) * 512],
                        s=(fc == 0), ee=(fc == 15):
                        t.matmul(pv, l, r, start=s, stop=ee))
                pr.tgroup(emits, reads=[B["w2s"], B["hS"]], writes=[pvB])
                dst = acc_sb[:, oc * OWN + tt * 512: oc * OWN + tt * 512 + 512]
                pr.add("vector", lambda v, pv=pv, dst=dst: v.scalar_tensor_tensor(
                    dst, pv, 1.0, dst, op0=ALU.mult, op1=ALU.add),
                    reads=[pvB, B["acc"]], writes=[B["acc"]])

    es_s2w.close()
    es_s2a.close()

    pr.new_seg()

    xn_tm_sb = sb(es_s3, "xn_tm_sb", [128, 2 * (NTC_OWN * 128)], BF16)
    pet_sb = sb(es_s3, "pet_sb", [128, NTC_OWN * 512], BF16)
    pe_sb = sb(es_s3, "pe_sb", [128, SC * 512], BF16)
    gv_sb = sb(es_s3, "gv_sb", [128, 4])
    iota_sb = sb(es_s3, "iota_sb", [128, 512], mybir.dt.int16)
    sid_sb = sb(es_s3, "sid_sb", [128, 4])
    onehf_sb = sb(es_s3, "onehf_sb", [NE, NE * 128])
    w1_sb = sb(es_s3, "w1_sb", [128, 8 * 2048], BF16)
    w2_sb = sb(es_s3, "w2_sb", [128, 16 * 1024], BF16)
    x_sb = pool_sb[:, 8192: 8192 + 8 * 512]
    osm_sb = pool_sb[:, 12288: 12288 + SC * 1024]

    psA3 = ps(es_s3, "psA3", [128, 1024])
    psB3 = ps(es_s3, "psB3", [128, 1024])
    psC3 = ps(es_s3, "psC3", [128, 512])
    psD3 = ps(es_s3, "psD3", [128, 512])

    nb(*[f"pet{i}" for i in range(NTC_OWN)])
    nb("pex")
    nb("xntm0", "xntm1", "pe", "gv", "iota", "sid", "onehf",
       "w1", "w2", "h", "x", "osm",
       "p3A0", "p3A1", "p3B0", "p3B1", "p3C", "p3D")

    pr.dma("sync", iota_sb[:, :], slotiota, writes=[B["iota"]])
    pr.dma("sync", sid_sb[:, :], slotid, writes=[B["sid"]])
    pr.dma("sync", onehf_sb[:, :], onehf, writes=[B["onehf"]])

    rw1_r = rw1.rearrange("e (kc p) f -> e p kc f", p=128)
    rw2_r = rw2.rearrange("e (fc p) o -> e p fc o", p=128)

    DCW = NTC_OWN * 128

    def emit_pet(e):
        for tc in range(NTC_OWN):
            pr.add("vector", lambda v, tc=tc, e=e: v.tensor_scalar(
                pet_sb[:, tc * 512:(tc + 1) * 512], iota_sb[:, :],
                pos_tm_sb[:, tc * NE + e: tc * NE + e + 1], None, op0=ALU.is_equal),
                reads=[B["iota"], B["pos_tm"]], writes=[B[f"pet{tc}"]])

    emit_pet(0)
    for e in range(NE):
        pr.dma("gpsimd", w1_sb[:, :].rearrange("p (kc f) -> p kc f", kc=8),
               rw1_r[e], writes=[B["w1"]])
        pr.dma("gpsimd", w2_sb[:, :].rearrange("p (fc o) -> p fc o", fc=16),
               rw2_r[e], writes=[B["w2"]])
        for sc in range(SC):
            emits = []
            for tc in range(NTC_OWN):
                emits.append(
                    lambda t, sc=sc, tc=tc, e=e:
                    t.matmul(psB3[:, sc:sc + 1],
                             pet_sb[:, tc * 512 + sc * 128: tc * 512 + sc * 128 + 128],
                             gated_tm_sb[:, tc * NE + e: tc * NE + e + 1],
                             start=(tc == 0), stop=(tc == NTC_OWN - 1)))
            pr.tgroup(emits, reads=[B[f"pet{tc}"] for tc in range(NTC_OWN)]
                      + [B["gated_tm"]], writes=[B["p3B0"]])
        pr.add("vector", lambda v: v.tensor_copy(gv_sb[:, 0:SC], psB3[:, 0:SC]),
               reads=[B["p3B0"]], writes=[B["gv"]])
        for dc in range(8):
            sl = dc % 2
            xnsl = xn_tm_sb[:, sl * DCW:(sl + 1) * DCW]
            xnB = B[f"xntm{sl}"]
            pr.dma("sync", xnsl, xn_dram[dc],
                   reads=[B["xndram"]], writes=[xnB])
            i = dc % 2
            pg = (psC3 if i == 0 else psD3)[:, 0:512]
            pgB = B["p3C"] if i == 0 else B["p3D"]
            emits = []
            for tc in range(NTC_OWN):
                emits.append(
                    lambda t, pg=pg, xnsl=xnsl, tc=tc:
                    t.matmul(pg, xnsl[:, tc * 128:(tc + 1) * 128],
                             pet_sb[:, tc * 512:(tc + 1) * 512],
                             start=(tc == 0), stop=(tc == NTC_OWN - 1)))
            pr.tgroup(emits, reads=[xnB] + [B[f"pet{tc}"] for tc in range(NTC_OWN)],
                      writes=[pgB])
            pexdeps = list(B["pex"].rs.values()) + (
                [B["pex"].w] if B["pex"].w is not None else [])
            pr.add("vector", lambda v, pg=pg, dc=dc: v.tensor_copy(
                x_sb[:, dc * 512:(dc + 1) * 512], pg),
                reads=[pgB], writes=[B["x"]], deps=pexdeps)
        if e + 1 < NE:
            emit_pet(e + 1)
        for fc in range(16):
            i = fc % 2
            ph = (psC3 if i == 0 else psD3)[:, 0:512]
            phB = B["p3C"] if i == 0 else B["p3D"]
            emits = []
            for kc in range(8):
                emits.append(
                    lambda t, ph=ph,
                    l=w1_sb[:, kc * 2048 + fc * 128: kc * 2048 + fc * 128 + 128],
                    r=x_sb[:, kc * 512:(kc + 1) * 512],
                    s=(kc == 0), ee=(kc == 7):
                    t.matmul(ph, l, r, start=s, stop=ee))
            pr.tgroup(emits, reads=[B["w1"], B["x"]], writes=[phB])
            b1col = rb1_sb[:, e * 16 + fc: e * 16 + fc + 1]
            hdst = h_sb[:, fc * 512:(fc + 1) * 512]
            pr.add("scalar", lambda s, ph=ph, hdst=hdst, b1col=b1col:
                   s.activation(hdst, ph, AF.Gelu, bias=b1col),
                   reads=[phB, B["rb1"]], writes=[B["h"]])
        for sc in range(SC):
            for dh in range(2):
                i = (sc * 2 + dh) % 2
                pv = (psC3 if i == 0 else psD3)[:, 0:512]
                pvB = B["p3C"] if i == 0 else B["p3D"]
                emits = []
                for fc in range(16):
                    emits.append(
                        lambda t, pv=pv,
                        l=h_sb[:, fc * 512 + sc * 128: fc * 512 + sc * 128 + 128],
                        r=w2_sb[:, fc * 1024 + dh * 512: fc * 1024 + dh * 512 + 512],
                        s=(fc == 0), ee=(fc == 15):
                        t.matmul(pv, l, r, start=s, stop=ee))
                pr.tgroup(emits, reads=[B["h"], B["w2"]], writes=[pvB])
                odst = osm_sb[:, sc * 1024 + dh * 512: sc * 1024 + dh * 512 + 512]
                pr.add("scalar", lambda s, pv=pv, odst=odst, sc=sc: s.activation(
                    odst, pv, AF.Identity, scale=gv_sb[:, sc:sc + 1]),
                    reads=[pvB, B["gv"]], writes=[B["osm"]])
        pe_alt = x_sb
        def build_pe(tt, e=e):
            par_ = tt % 2
            slot = pe_sb if par_ == 0 else pe_alt
            peB = B["pe"] if par_ == 0 else B["pex"]
            pb = psB3[:, 512:1024] if par_ == 0 else psB3[:, 0:512]
            pbB = B["p3B1"] if par_ == 0 else B["p3B0"]
            extra = []
            if par_ == 1:
                xb = B["x"]
                extra = list(xb.rs.values()) + ([xb.w] if xb.w is not None else [])
            pr.add("tensor", lambda t, e=e, tt=tt, pb=pb: t.matmul(
                pb, onehf_sb[0:NE, e * 128:(e + 1) * 128],
                pm_sb[0:NE, tt * 512: tt * 512 + 512], start=True, stop=True),
                reads=[B["onehf"], B["PM"]], writes=[pbB])
            for sc in range(SC):
                pr.add("vector", lambda v, sc=sc, slot=slot, pb=pb: v.tensor_scalar(
                    slot[:, sc * 512:(sc + 1) * 512], pb,
                    sid_sb[:, sc:sc + 1], None, op0=ALU.is_equal),
                    reads=[pbB, B["sid"]], writes=[peB], deps=extra)

        def scatter_tt(tt):
            par_ = tt % 2
            slot = pe_sb if par_ == 0 else pe_alt
            peB = B["pe"] if par_ == 0 else B["pex"]
            for dc in range(8):
                i = dc % 2
                pv = psA3[:, i * 512: i * 512 + 512]
                pvB = B["p3A0"] if i == 0 else B["p3A1"]
                emits = []
                for sc in range(SC):
                    emits.append(
                        lambda t, pv=pv,
                        l=osm_sb[:, sc * 1024 + dc * 128: sc * 1024 + dc * 128 + 128],
                        r=slot[:, sc * 512:(sc + 1) * 512],
                        s=(sc == 0), ee=(sc == SC - 1):
                        t.matmul(pv, l, r, start=s, stop=ee))
                pr.tgroup(emits, reads=[B["osm"], peB], writes=[pvB])
                dst = acc_sb[:, dc * OWN + tt * 512: dc * OWN + tt * 512 + 512]
                pr.add("vector", lambda v, pv=pv, dst=dst: v.scalar_tensor_tensor(
                    dst, pv, 1.0, dst, op0=ALU.mult, op1=ALU.add),
                    reads=[pvB, B["acc"]], writes=[B["acc"]])

        build_pe(0)
        for tt in range(NTT):
            if tt + 1 < NTT:
                build_pe(tt + 1)
            scatter_tt(tt)

    out_r = out_fm.rearrange("(oc p) t -> p oc t", p=128)
    pr.dma("sync", out_r, acc_sb[:, :].rearrange("p (oc t) -> p oc t", oc=8),
           reads=[B["acc"]])

    totals = pr.finalize_and_emit(nc, sems)
    pr.emit_blocks(nc, sems)

    with nc.Block("fin") as block:
        def fin(s):
            s.wait_ge(sems["sync"], totals["sync"])
            s.wait_ge(sems["gpsimd"], totals["gpsimd"])
        block.sync(fin)

    es_s3.close()
    es_acc.close()
    es_ctx.close()
    es_global.close()
    es_sem.close()
    return nc



def host_prep(x, wq, wk, wv, wo, b_o, sw1, sb1, sw2, sb2, rw1, rb1, rw2,
              rb2, gw, gb, cfg):
    T, NH, NE = cfg["T"], cfg["NH"], cfg["NE"]
    OWN = T // 2
    bf = ml_dtypes.bfloat16
    f32 = np.float32

    x = np.ascontiguousarray(np.asarray(x, dtype=f32))
    nseq = x.shape[0]

    def c32(a):
        return np.ascontiguousarray(np.asarray(a, dtype=f32))

    wq, wk, wv, wo = c32(wq), c32(wk), c32(wv), c32(wo)
    sw1, sw2 = c32(sw1).reshape(1024, 4096), c32(sw2).reshape(4096, 1024)
    rw1, rw2 = c32(rw1), c32(rw2)
    rb2a = c32(rb2).reshape(NE, 1024)
    gwa = c32(gw)
    gba = c32(gb).reshape(NE)
    b_o = c32(b_o).reshape(1024)
    sb1a = c32(sb1).reshape(4096)
    sb2a = c32(sb2).reshape(1024)
    rb1a = c32(rb1).reshape(NE, 2048)

    gw_pp = np.ascontiguousarray(
        gwa.reshape(8, 128, NE).transpose(1, 0, 2).reshape(128, 8 * NE))
    gb_bc = np.ascontiguousarray(np.broadcast_to(gba[None, :], (128, NE)))
    bo_pp = np.ascontiguousarray(b_o.reshape(8, 128).T)
    sb2_pp = np.ascontiguousarray(sb2a.reshape(8, 128).T)
    sb1_pp = np.ascontiguousarray(sb1a.reshape(32, 128).T)
    rb1_pp = np.ascontiguousarray(
        rb1a.reshape(NE, 16, 128).transpose(2, 0, 1).reshape(128, NE * 16))

    kk = np.arange(128)[:, None]
    jj = np.arange(512)[None, :]
    identb = np.eye(128, dtype=bf)
    identf = np.eye(128, dtype=f32)
    onehf = np.zeros((NE, NE * 128), dtype=f32)
    for e in range(NE):
        onehf[e, e * 128:(e + 1) * 128] = 1.0
    slotiota = np.tile(np.arange(512, dtype=np.int16), (128, 1))
    slotid = (np.arange(4, dtype=f32)[None, :] * 128
              + np.arange(128, dtype=f32)[:, None])
    slotid = np.ascontiguousarray(slotid)

    shared = dict(wq=wq, wk=wk, wv=wv, wo=wo, rb2=rb2a, gw_pp=gw_pp,
                  gb_bc=gb_bc, bo_pp=bo_pp, sb2_pp=sb2_pp, sb1_pp=sb1_pp,
                  rb1_pp=rb1_pp, sw1=sw1, sw2=sw2, rw1=rw1, rw2=rw2,
                  identb=identb, identf=identf, onehf=onehf,
                  slotiota=slotiota, slotid=slotid)

    in_maps = []
    for c in range(2 * nseq):
        s, r = c // 2, c % 2
        xo = np.ascontiguousarray(x[s, r * OWN:(r + 1) * OWN])
        xkv = np.ascontiguousarray(
            x[s].reshape(T // 128, 128, 1024)[r::2].reshape(OWN, 1024))
        m = np.concatenate(
            [np.tile(np.where(kk + 128 * (2 * i + r) > jj, 0.0, 1.0), (1, 2))
             for i in range(2)], axis=1)
        m_ = dict(shared)
        m_.update(x_all=x[s], x_own=xo, x_kv=xkv,
                  x_own_fm=np.ascontiguousarray(xo.T),
                  masks=m.astype(bf))
        in_maps.append(m_)
    return in_maps


def kernel(L0_attn_x, L0_attn_W_q, L0_attn_W_k, L0_attn_W_v, L0_attn_W_o,
           L0_attn_b_o, L0_ffn_sW1, L0_ffn_sb1, L0_ffn_sW2, L0_ffn_sb2,
           L0_ffn_rW1, L0_ffn_rb1, L0_ffn_rW2, L0_ffn_rb2, L0_ffn_gW,
           L0_ffn_gb):
    cfg = full_cfg()
    in_maps = host_prep(L0_attn_x, L0_attn_W_q, L0_attn_W_k, L0_attn_W_v,
                        L0_attn_W_o, L0_attn_b_o, L0_ffn_sW1, L0_ffn_sb1,
                        L0_ffn_sW2, L0_ffn_sb2, L0_ffn_rW1, L0_ffn_rb1,
                        L0_ffn_rW2, L0_ffn_rb2, L0_ffn_gW, L0_ffn_gb, cfg)
    nc = build_nc(cfg)
    res = run_bass_kernel_spmd(nc, in_maps, core_ids=list(range(8)))
    OWN = cfg["T"] // 2
    out = np.empty((4, cfg["T"], 1024), dtype=np.float32)
    for c in range(8):
        s, r = c // 2, c % 2
        out[s, r * OWN:(r + 1) * OWN] = np.asarray(res.results[c]["out_fm"]).T
    return out


# version 42
# speedup vs baseline: 1.0701x; 1.0701x over previous
import numpy as np
import ml_dtypes
from contextlib import ExitStack

import concourse.bass as bass
import concourse.mybir as mybir
from concourse.bass_utils import run_bass_kernel_spmd

F32 = mybir.dt.float32
BF16 = mybir.dt.bfloat16
AF = mybir.ActivationFunctionType
ALU = mybir.AluOpType
AX = mybir.AxisListType

ENGINES = ("tensor", "vector", "scalar", "gpsimd", "sync")
COUNTERS = ENGINES + ("cc", "pool", "sdma")
MASKVAL = -100000.0


def full_cfg():
    return dict(T=4096, NH=16, NE=15)



class Op:
    __slots__ = ("eng", "emit", "deps", "dma", "cc", "tok", "need")

    def __init__(self, eng, emit, deps, dma, cc=False):
        self.eng, self.emit, self.deps, self.dma = eng, emit, deps, dma
        self.cc = cc
        self.tok, self.need = None, False


class Buf:
    __slots__ = ("w", "rs")

    def __init__(self):
        self.w = None
        self.rs = {}


class Prog:
    def __init__(self):
        self.segs = [[]]
        self.ops = []

    def new_seg(self):
        self.segs.append([])

    def _deps_for(self, reads, writes, deps):
        dd = [d for d in deps if d is not None]
        for b in reads:
            if b.w is not None:
                dd.append(b.w)
        for b in writes:
            if b.w is not None:
                dd.append(b.w)
            dd.extend(b.rs.values())
        return dd

    def _register(self, op_first, op_last, reads, writes):
        for b in reads:
            b.rs[op_last.eng] = op_last
        for b in writes:
            b.w = op_last
            b.rs = {}

    def add(self, eng, emit, reads=(), writes=(), deps=(), dma=False, cc=False):
        o = Op(eng, emit, self._deps_for(reads, writes, deps), dma, cc)
        self.segs[-1].append(o)
        self.ops.append(o)
        self._register(o, o, reads, writes)
        return o

    def tgroup(self, emits, reads=(), writes=(), deps=()):
        first = Op("tensor", emits[0], self._deps_for(reads, writes, deps), False)
        self.segs[-1].append(first)
        self.ops.append(first)
        last = first
        for e in emits[1:]:
            last = Op("tensor", e, [], False)
            self.segs[-1].append(last)
            self.ops.append(last)
        self._register(first, last, reads, writes)
        return last

    def dma(self, eng, out, in_, reads=(), writes=(), deps=()):
        def emit(e, out=out, in_=in_):
            return e.dma_start(out=out, in_=in_)
        return self.add(eng, emit, reads=reads, writes=writes, deps=deps, dma=True)

    def finalize_and_emit(self, nc, sems):
        for o in self.ops:
            for d in o.deps:
                d.need = True
            if o.dma:
                o.need = True
        for seg in self.segs:
            last = {}
            for o in seg:
                last[o.eng] = o
            for o in last.values():
                o.need = True
        cnt = {e: 0 for e in COUNTERS}
        seg_end = []
        for seg in self.segs:
            for o in seg:
                if o.need:
                    if o.cc:
                        key = "cc"
                    elif o.eng == "gpsimd" and not o.dma:
                        key = "pool"
                    elif o.eng == "scalar" and o.dma:
                        key = "sdma"
                    else:
                        key = o.eng
                    cnt[key] += 16 if o.dma else 1
                    o.tok = (key, cnt[key])
            seg_end.append(dict(cnt))

        self.seg_end = seg_end
        return seg_end[-1]

    def emit_blocks(self, nc, sems):
        seg_end = self.seg_end
        water = {e: {f: 0 for f in COUNTERS} for e in ENGINES}
        for si, seg in enumerate(self.segs):
            eng_ops = {e: [o for o in seg if o.eng == e] for e in ENGINES}
            if not any(eng_ops.values()):
                continue
            with nc.Block(f"seg{si}") as block:
                def mk_body(e):
                    def body(eng, e=e, si=si):
                        if si > 0:
                            prev = seg_end[si - 1]
                            for f in COUNTERS:
                                if f != e and prev[f] > water[e][f]:
                                    eng.wait_ge(sems[f], prev[f])
                                    water[e][f] = prev[f]
                        for o in eng_ops[e]:
                            req = {}
                            if o.dma and o.tok is not None:
                                req[o.tok[0]] = o.tok[1] - 16
                            for d in o.deps:
                                if d.tok is not None:
                                    fe, v = d.tok
                                    if v > req.get(fe, 0):
                                        req[fe] = v
                            for fe, v in req.items():
                                if v > water[e][fe]:
                                    eng.wait_ge(sems[fe], v)
                                    water[e][fe] = v
                            inst = o.emit(eng)
                            if o.tok is not None:
                                inst.then_inc(sems[o.tok[0]], 16 if o.dma else 1)
                    return body
                for e in ENGINES:
                    getattr(block, e)(mk_body(e))



def build_nc(cfg):
    T = cfg["T"]
    NH = cfg["NH"]
    NE = cfg["NE"]
    OWN = T // 2
    HG = NH // 4
    NTC_ALL = T // 128
    NTC_OWN = OWN // 128
    NQT = T // 512
    NQO = OWN // 512
    NHH = NH // 2
    NTT = OWN // 512
    CAP = 512
    SC = CAP // 128

    nc = bass.Bass("TRN2", target_bir_lowering=False, debug=False)

    def din(name, shape, dt=F32):
        return nc.dram_tensor(name, shape, dt, kind="ExternalInput").ap()

    x_all = din("x_all", [T, 1024])
    x_own = din("x_own", [OWN, 1024])
    x_own_fm = din("x_own_fm", [1024, OWN])
    x_kv = din("x_kv", [OWN, 1024])
    wq = din("wq", [1024, 1024])
    wk = din("wk", [1024, 1024])
    wv = din("wv", [1024, 1024])
    wo = din("wo", [1024, 1024])
    rb2 = din("rb2", [NE, 1024])
    gw_pp = din("gw_pp", [128, 8 * NE])
    gb_bc = din("gb_bc", [128, NE])
    bo_pp = din("bo_pp", [128, 8])
    sb2_pp = din("sb2_pp", [128, 8])
    sb1_pp = din("sb1_pp", [128, 32])
    rb1_pp = din("rb1_pp", [128, NE * 16])
    sw1 = din("sw1", [1024, 4096])
    sw2 = din("sw2", [4096, 1024])
    rw1 = din("rw1", [NE, 1024, 2048])
    rw2 = din("rw2", [NE, 2048, 1024])
    masks = din("masks", [128, 2 * 1024], BF16)
    slotiota = din("slotiota", [128, 512], mybir.dt.int16)
    slotid = din("slotid", [128, 4])
    identb = din("identb", [128, 128], BF16)
    identf = din("identf", [128, 128])
    onehf = din("onehf", [NE, NE * 128])
    out_fm = nc.dram_tensor("out_fm", [1024, OWN], F32, kind="ExternalOutput").ap()
    xn_dram = nc.dram_tensor("xn_tm_scratch", [8, 128, OWN], BF16,
                             kind="Internal").ap()
    cc_in = nc.dram_tensor("cc_in", [HG, 2, 4 * 65, OWN], F32).ap()
    cc_out = nc.dram_tensor("cc_out", [HG, 4 * 65, OWN], F32).ap()

    es_global = ExitStack()
    es_ctx = ExitStack()
    es_xnfm = ExitStack()
    es_s1a = ExitStack()
    es_s1b = ExitStack()
    es_s1c = ExitStack()
    es_acc = ExitStack()
    es_s2a = ExitStack()
    es_s2b = ExitStack()
    es_s2w = ExitStack()
    es_s3 = ExitStack()
    es_sem = ExitStack()
    sems = {e: es_sem.enter_context(nc.semaphore(f"ctr_{e}")) for e in COUNTERS}

    def sb(es, name, shape, dt=F32):
        return es.enter_context(nc.sbuf_tensor(name, shape, dt))

    def ps(es, name, shape, dt=F32):
        return es.enter_context(nc.psum_tensor(name, shape, dt))

    pr = Prog()
    B = {}

    def nb(*names):
        for n in names:
            B[n] = Buf()

    identb_sb = sb(es_global, "identb_sb", [128, 128], BF16)
    ones_sb = sb(es_global, "ones_sb", [1, 128], BF16)

    eps_sb = sb(es_global, "eps_sb", [128, 1])
    bo_sb = sb(es_global, "bo_sb", [128, 8])
    sb2_sb = sb(es_global, "sb2_sb", [128, 8])
    sb1_sb = sb(es_global, "sb1_sb", [128, 32])
    rb1_sb = sb(es_global, "rb1_sb", [128, NE * 16])

    nb("masks", "identb", "identf", "ones", "oneh", "eps", "gw", "gb",
       "rb2", "bo", "sb2", "sb1", "rb1", "gfm", "slotiota", "slotid",
       "gated_tm", "pos_tm", "PM", "xndram")

    pr.dma("sync", identb_sb[:, :], identb, writes=[B["identb"]])
    pr.add("vector", lambda v: v.memset(ones_sb[:, :], 1.0), writes=[B["ones"]])
    pr.dma("sync", bo_sb[:, :], bo_pp, writes=[B["bo"]])
    pr.dma("sync", sb2_sb[:, :], sb2_pp, writes=[B["sb2"]])
    pr.dma("sync", sb1_sb[:, :], sb1_pp, writes=[B["sb1"]])
    pr.dma("sync", rb1_sb[:, :], rb1_pp, writes=[B["rb1"]])
    pr.add("vector", lambda v: v.memset(eps_sb[:, :], 1e-5), writes=[B["eps"]])

    ctx_sb = sb(es_ctx, "ctx_sb", [128, 8 * OWN], BF16)
    nb("ctx")
    if NH < 16:
        pr.add("vector", lambda v: v.memset(ctx_sb[:, :], 0.0),
               writes=[B["ctx"]])

    xnfm_sb = sb(es_xnfm, "xnfm_sb", [128, 8 * T], BF16)
    xnkv_sb = sb(es_xnfm, "xnkv_sb", [128, 8 * OWN], BF16)
    nb("xnfm", "xnkv")

    xt_sb = sb(es_s1a, "xt_sb", [128, 4 * 1024])
    xnt_sb = sb(es_s1a, "xnt_sb", [128, 4 * 1024], BF16)
    stat_sb = sb(es_s1a, "stat_sb", [128, 4 * 12])
    mv_sb = sb(es_s1a, "mv_sb", [128, 4 * 2])
    rstd_sb = sb(es_s1a, "rstd_sb", [128, 4])
    negmr_sb = sb(es_s1a, "negmr_sb", [128, 4])
    psE = [ps(es_s1a, f"psE{i}", [128, 1024], BF16) for i in range(4)]

    nb("xt0", "xt1", "xt2", "xt3", "xnt0", "xnt1", "xnt2", "xnt3",
       "stat0", "stat1", "stat2", "stat3", "mv0", "mv1", "mv2", "mv3",
       "rstd0", "rstd1", "rstd2", "rstd3",
       "negmr0", "negmr1", "negmr2", "negmr3",
       "pE0", "pE1", "pE2", "pE3")

    def emit_ln_a(tc, x_tiled, pfx, xts_t, stat_t, mv_t, rstd_t, negmr_t,
                  depth=4, qalt=False):
        par = tc % depth
        xtB = B[f"{pfx}xt{par}"]
        stB, mvB = B[f"{pfx}stat{par}"], B[f"{pfx}mv{par}"]
        rsB, nmB = B[f"{pfx}rstd{par}"], B[f"{pfx}negmr{par}"]
        xts = xts_t[:, par * 1024:(par + 1) * 1024]
        q = "scalar" if (qalt and tc % 2 == 1) else "sync"
        pr.dma(q, xts, x_tiled[tc], writes=[xtB])
        st = stat_t[:, par * 12:(par + 1) * 12]
        pr.add("vector", lambda v, xts=xts, st=st: v.bn_stats(st[:, 0:6], xts[:, 0:512]),
               reads=[xtB], writes=[stB])
        pr.add("vector", lambda v, xts=xts, st=st: v.bn_stats(st[:, 6:12], xts[:, 512:1024]),
               reads=[xtB], writes=[stB])
        mv = mv_t[:, par * 2:(par + 1) * 2]
        pr.add("vector", lambda v, st=st, mv=mv: v.bn_aggr(mv, st),
               reads=[stB], writes=[mvB])
        rs = rstd_t[:, par:par + 1]
        pr.add("scalar", lambda s, mv=mv, rs=rs: s.activation(
            rs, mv[:, 1:2], AF.Ln, bias=eps_sb[:, 0:1]),
            reads=[mvB, B["eps"]], writes=[rsB])
        pr.add("scalar", lambda s, rs=rs: s.activation(rs, rs, AF.Exp, scale=-0.5),
               reads=[rsB], writes=[rsB])

    def emit_ln_b(tc, pfx, xts_t, xnts_t, rstd_t, negmr_t, depth=4,
                  mv_t=None, stat_t=None):
        par = tc % depth
        xtB, xntB = B[f"{pfx}xt{par}"], B[f"{pfx}xnt{par}"]
        rsB, nmB = B[f"{pfx}rstd{par}"], B[f"{pfx}negmr{par}"]
        mvB = B[f"{pfx}mv{par}"]
        xts = xts_t[:, par * 1024:(par + 1) * 1024]
        rs = rstd_t[:, par:par + 1]
        nm = negmr_t[:, par:par + 1]
        mv = mv_t[:, par * 2:(par + 1) * 2]
        pr.add("vector", lambda v, mv=mv, rs=rs, nm=nm: v.tensor_scalar(
            nm, mv[:, 0:1], rs, -1.0, op0=ALU.mult, op1=ALU.mult),
            reads=[mvB, rsB], writes=[nmB])
        xnts = xnts_t[:, par * 1024:(par + 1) * 1024]
        pr.add("scalar", lambda s, xts=xts, xnts=xnts, rs=rs, nm=nm: s.activation(
            xnts, xts, AF.Identity, scale=rs, bias=nm),
            reads=[xtB, rsB, nmB], writes=[xntB])
        return xnts, xntB

    x_all_t = x_all.rearrange("(tc p) d -> tc p d", p=128)
    x_kv_t = x_kv.rearrange("(tc p) d -> tc p d", p=128)
    for src_t, n_t, dst_t, dstB, W in ((x_all_t, NTC_ALL, xnfm_sb, B["xnfm"], T),
                                       (x_kv_t, NTC_OWN, xnkv_sb, B["xnkv"], OWN)):
        for it in range(n_t + 2):
            if it >= 2:
                tc = it - 2
                xnts, xntB = emit_ln_b(tc, "", xt_sb, xnt_sb, rstd_sb, negmr_sb,
                                       mv_t=mv_sb)
            if it < n_t:
                emit_ln_a(it, src_t, "", xt_sb, stat_sb, mv_sb, rstd_sb,
                          negmr_sb, qalt=True)
            if it < 2:
                continue
            for j in range(8):
                pb = B[f"pE{j % 4}"]
                pt = psE[j % 4][:, 0:128]
                pr.add("tensor", lambda t, pt=pt, src=xnts[:, j * 128:(j + 1) * 128]:
                       t.transpose(pt, src, identb_sb[:, :]),
                       reads=[xntB, B["identb"]], writes=[pb])
                dst = dst_t[:, j * W + tc * 128: j * W + tc * 128 + 128]
                pr.add("vector", lambda v, pt=pt, dst=dst: v.tensor_copy(dst, pt),
                       reads=[pb], writes=[dstB])

    es_s1a.close()

    pr.new_seg()

    KT = OWN
    NKC = KT // 128

    masks_sb = sb(es_s1b, "masks_sb", [128, 2 * 1024], BF16)
    q_sb = sb(es_s1b, "q_sb", [128, 2 * T], BF16)
    k_sb = sb(es_s1b, "k_sb", [128, 2 * KT], BF16)
    v_sb = sb(es_s1b, "v_sb", [128, NKC * 260], BF16)
    pt_sb = sb(es_s1b, "pt_sb", [128, 5 * 1024], BF16)
    wq_sb = sb(es_s1b, "wq_sb", [128, 8 * 256], BF16)
    wk_sb = sb(es_s1b, "wk_sb", [128, 8 * 256], BF16)
    wv_sb = sb(es_s1b, "wv_sb", [128, 8 * 256], BF16)
    stg_sb = sb(es_s1b, "stg_sb", [128, 2 * 1024])

    psA = ps(es_s1b, "psA", [128, 1024])
    psB = ps(es_s1b, "psB", [128, 1024])
    psG = ps(es_s1b, "psG", [128, 1024])
    psC = ps(es_s1b, "psC", [128, 512])
    psD = ps(es_s1b, "psD", [128, 512])

    pr.dma("sync", masks_sb[:, :], masks, writes=[B["masks"]])

    for k_ in range(HG):
        nb(f"ccin{k_}", f"ccout{k_}")
    nb("q", "k", "v", "wq", "wk", "wv", "stg0", "stg1",
       "pt0", "pt1", "pt2", "pt3", "pt4",
       "pA0", "pA1", "pB0", "pB1", "pG", "pC", "pD")

    wq_r = wq.rearrange("(kc p) o -> p kc o", p=128)
    wk_r = wk.rearrange("(kc p) o -> p kc o", p=128)
    wv_r = wv.rearrange("(kc p) o -> p kc o", p=128)

    it_hp = [0]
    gbase = [0]
    pend_tail = [None]

    def flush_tail():
        if pend_tail[0] is not None:
            fn, pend_tail[0] = pend_tail[0], None
            fn()

    def emit_cc(k):
        pr.add("gpsimd", lambda g, k=k: g.collective_compute(
            "ReduceScatter", ALU.add,
            replica_groups=[[2 * i, 2 * i + 1] for i in range(4)],
            ins=[cc_in[k:k + 1].rearrange("a q r t -> (a q r) t").opt()],
            outs=[cc_out[k:k + 1].rearrange("a r t -> (a r) t").opt()]),
            reads=[B[f"ccin{k}"]], writes=[B[f"ccout{k}"]], cc=True)

    for hg in range(HG):
        flush_tail()
        for wsb, wr, bn in ((wq_sb, wq_r, "wq"), (wk_sb, wk_r, "wk"),
                            (wv_sb, wv_r, "wv")):
            pr.dma("gpsimd",
                   wsb[:, :].rearrange("p (kc o) -> p kc o", kc=8),
                   wr[:, :, hg * 256:(hg + 1) * 256],
                   writes=[B[bn]])
        if hg >= 1:
            emit_cc(hg - 1)
        for dst_sb, dstB, wsb, wB, src_sb, srcB, W, pst, pb0, pb1 in (
                (q_sb, "q", wq_sb, "wq", xnfm_sb, B["xnfm"], T, psA, "pA0", "pA1"),
                (k_sb, "k", wk_sb, "wk", xnkv_sb, B["xnkv"], KT, psB, "pB0", "pB1")):
            for hp in range(2):
                for qt in range(W // 512):
                    i = qt % 2
                    pv = pst[:, i * 512: i * 512 + 512]
                    pvB = B[pb0] if i == 0 else B[pb1]
                    emits = []
                    for kc in range(8):
                        emits.append(
                            lambda t, pv=pv,
                            l=wsb[:, kc * 256 + hp * 128: kc * 256 + hp * 128 + 128],
                            r=src_sb[:, kc * W + qt * 512: kc * W + qt * 512 + 512],
                            s=(kc == 0), e=(kc == 7):
                            t.matmul(pv, l, r, start=s, stop=e))
                    pr.tgroup(emits, reads=[B[wB], srcB], writes=[pvB])
                    dst = dst_sb[:, hp * W + qt * 512: hp * W + qt * 512 + 512]
                    pr.add("vector", lambda v, pv=pv, dst=dst: v.tensor_copy(dst, pv),
                           reads=[pvB], writes=[B[dstB]])
        pr.add("vector", lambda v: v.memset(
            v_sb[:, :].rearrange("p (c h d) -> p c h d", c=NKC, h=4)[:, :, :, 64:65],
            1.0), writes=[B["v"]])
        for tkc in range(NKC):
            i = tkc % 2
            pv = (psC if i == 0 else psD)[:, 0:256]
            pvB = B["pC"] if i == 0 else B["pD"]
            emits = []
            for kc in range(8):
                emits.append(
                    lambda t, pv=pv,
                    l=xnkv_sb[:, kc * KT + tkc * 128: kc * KT + tkc * 128 + 128],
                    r=wv_sb[:, kc * 256: kc * 256 + 256],
                    s=(kc == 0), e=(kc == 7):
                    t.matmul(pv, l, r, start=s, stop=e))
            pr.tgroup(emits, reads=[B["wv"], B["xnkv"]], writes=[pvB])
            dst = v_sb[:, tkc * 260: tkc * 260 + 260].rearrange(
                "p (h d) -> p h d", d=65)[:, :, 0:64]
            src = pv.rearrange("p (h d) -> p h d", d=64)
            pr.add("vector", lambda v, src=src, dst=dst: v.tensor_copy(dst, src),
                   reads=[pvB], writes=[B["v"]])

        for hp in range(2):
            for p in range(NQT):
                par = it_hp[0] % 2
                it_hp[0] += 1
                ncl = 2 * p + 2
                base = gbase[0]
                gbase[0] += ncl
                psS_ring = (psA, psB, psG)
                sB_ring = ([B["pA0"], B["pA1"]], [B["pB0"], B["pB1"]], [B["pG"]])

                def emit_av(ca, cb, ncl=ncl, hp=hp, base=base):
                    for half, avp, avB in ((0, psC, B["pC"]), (1, psD, B["pD"])):
                        emits = []
                        for c in (ca, cb):
                            sl = (base + c) % 5
                            ptv = pt_sb[:, sl * 1024 + half * 512:
                                        sl * 1024 + half * 512 + 512]
                            emits.append(
                                lambda t, avp=avp,
                                l=v_sb[:, c * 260 + (2 * hp + half) * 65:
                                       c * 260 + (2 * hp + half) * 65 + 65],
                                r=ptv, st=(c == 0), e=(c == ncl - 1):
                                t.matmul(avp[0:65, :], l, r, start=st, stop=e))
                        pr.tgroup(emits,
                                  reads=[B["v"], B[f"pt{(base + ca) % 5}"],
                                         B[f"pt{(base + cb) % 5}"]],
                                  writes=[avB])

                for b in range(ncl):
                    psS = psS_ring[b % 3]
                    sB = sB_ring[b % 3]
                    emits = []
                    for half in (0, 1):
                        emits.append(
                            lambda t, pv=psS[:, half * 512:(half + 1) * 512],
                            l=k_sb[64 * half:64 * half + 64,
                                   hp * KT + b * 128: hp * KT + b * 128 + 128],
                            r=q_sb[64 * half:64 * half + 64,
                                   hp * T + p * 512: hp * T + p * 512 + 512]:
                            t.matmul(pv, l, r, start=True, stop=True))
                    pr.tgroup(emits, reads=[B["k"], B["q"]], writes=sB)
                    sl = (base + b) % 5
                    ptB = B[f"pt{sl}"]
                    ptv = pt_sb[:, sl * 1024:sl * 1024 + 1024]
                    pr.add("scalar", lambda sc, pv=psS[:, 0:1024], ptv=ptv:
                           sc.activation(ptv, pv, AF.Exp, scale=0.125),
                           reads=sB, writes=[ptB])
                    if b >= 2 * p:
                        i01 = b - 2 * p
                        pr.add("vector", lambda v, ptv=ptv,
                               m=masks_sb[:, i01 * 1024: i01 * 1024 + 1024]:
                               v.tensor_tensor(ptv, ptv, m, op=ALU.mult),
                               reads=[B["masks"], ptB], writes=[ptB])
                    if b == 1:
                        flush_tail()
                    if b >= 3 and b % 2 == 1:
                        emit_av(b - 3, b - 2)

                def tail(emit_av=emit_av, ncl=ncl, par=par, hg=hg, hp=hp, p=p):
                    emit_av(ncl - 2, ncl - 1)
                    stg = stg_sb[:, par * 1024:(par + 1) * 1024]
                    stgB = B[f"stg{par}"]
                    pr.add("vector", lambda v, stg=stg: v.tensor_copy(
                        stg[0:65, 0:512], psC[0:65, :]),
                        reads=[B["pC"]], writes=[stgB])
                    pr.add("vector", lambda v, stg=stg: v.tensor_copy(
                        stg[0:65, 512:1024], psD[0:65, :]),
                        reads=[B["pD"]], writes=[stgB])
                    h_even = hg * 4 + 2 * hp
                    qh, pl = divmod(p, NQT // 2)
                    for half in (0, 1):
                        h_ = h_even + half
                        hh_, hl_ = divmod(h_, 4)
                        pr.dma("sync",
                               cc_in[hh_, qh, hl_ * 65:(hl_ + 1) * 65,
                                     pl * 512: pl * 512 + 512],
                               stg[0:65, half * 512: half * 512 + 512],
                               reads=[stgB], writes=[B[f"ccin{hh_}"]])
                pend_tail[0] = tail

    flush_tail()
    es_s1b.close()

    es_xnfm.close()

    pr.new_seg()

    acc_sb = sb(es_acc, "acc_sb", [128, 8 * OWN])
    gated_tm_sb = sb(es_acc, "gated_tm", [128, NTC_OWN * NE], BF16)
    pos_tm_sb = sb(es_acc, "pos_tm", [128, NTC_OWN * NE])
    pm_sb = sb(es_acc, "pm_sb", [NE, OWN])
    if 8 * OWN >= 16384:
        pool_sb = ctx_sb
    else:
        pool_sb = sb(es_acc, "pool_sb", [128, 16384], BF16)

    xnofm_sb = sb(es_s2a, "xnofm_sb", [128, 8 * OWN], BF16)

    wo_sb = sb(es_s2b, "wo_sb", [128, 8 * 1024], BF16)
    xt2_sb = sb(es_s2b, "xt2_sb", [128, 2 * 1024])
    xnt2_sb = sb(es_s2b, "xnt2_sb", [128, 2 * 1024])
    xnbf_sb = sb(es_s2b, "xnbf_sb", [128, 2 * 1024], BF16)
    stat2_sb = sb(es_s2b, "stat2_sb", [128, 2 * 12])
    mv2_sb = sb(es_s2b, "mv2_sb", [128, 2 * 2])
    rstd2_sb = sb(es_s2b, "rstd2_sb", [128, 2])
    negmr2_sb = sb(es_s2b, "negmr2_sb", [128, 2])
    gwork_sb = sb(es_s2b, "gwork_sb", [128, 2 * 7 * 16])
    gsc_sb = sb(es_s2b, "gsc_sb", [128, 2 * 16])
    gpad_sb = sb(es_s2b, "gpad_sb", [128, 2 * 128], BF16)
    gfm_sb = sb(es_s2b, "gfm_sb", [NE, OWN], BF16)
    mfm_sb = sb(es_s2b, "mfm_sb", [NE, OWN], BF16)
    identf_sb = sb(es_s2b, "identf_sb", [128, 128])
    rstg_sb = sb(es_s2b, "rstg_sb", [64, OWN])
    lpf_sb = sb(es_s2b, "lpf_sb", [4, OWN // 2])
    lpb_sb = sb(es_s2b, "lpb_sb", [4, OWN], BF16)
    lr_sb = sb(es_s2b, "lr_sb", [128, OWN], BF16)
    onesb_sb = sb(es_s2b, "onesb_sb", [128, 64], BF16)
    gw_sb = sb(es_s2b, "gw_sb", [128, 8 * NE], BF16)
    gb_sb = sb(es_s2b, "gb_sb", [128, NE])
    rb2_sb = sb(es_s2b, "rb2_sb", [NE, 1024], BF16)

    psA2 = ps(es_s2b, "psA2", [128, 1024])
    psB2 = ps(es_s2b, "psB2", [128, 1024])
    psC2 = ps(es_s2b, "psC2", [128, 512])
    psD2 = ps(es_s2b, "psD2", [128, 512])
    psG2 = ps(es_s2b, "psG2", [128, 1024], BF16)
    psH2 = ps(es_s2b, "psH2", [128, 512])

    nb("acc", "xnofm", "wo", "2xt0", "2xt1", "2xt2", "2xt3",
       "2xnt0", "2xnt1", "2xnt2", "2xnt3",
       "xnbf0", "xnbf1", "xnbf2", "xnbf3",
       "2stat0", "2stat1", "2stat2", "2stat3", "2mv0", "2mv1", "2mv2", "2mv3",
       "2rstd0", "2rstd1", "2rstd2", "2rstd3",
       "2negmr0", "2negmr1", "2negmr2", "2negmr3",
       "gwork0", "gwork1", "gsc0", "gsc1", "gpad0", "gpad1", "mfm",
       "rstg", "lpf", "lr", "onesb",
       "lpb", "p2A0", "p2A1", "p2B0", "p2B1", "p2C", "p2D", "p2E", "p2F")

    pr.dma("sync", identf_sb[:, :], identf, writes=[B["identf"]])
    pr.dma("sync", gb_sb[:, :], gb_bc, writes=[B["gb"]])
    pr.dma("gpsimd", gw_sb[:, :], gw_pp, writes=[B["gw"]])

    x_own_fm_r = x_own_fm.rearrange("(oc p) t -> p oc t", p=128)
    acc_r = acc_sb[:, :].rearrange("p (oc t) -> p oc t", oc=8)
    pr.dma("scalar", acc_r[:, 0:4], x_own_fm_r[:, 0:4], writes=[B["acc"]])
    pr.dma("gpsimd", acc_r[:, 4:8], x_own_fm_r[:, 4:8], writes=[B["acc"]])

    wo_r = wo.rearrange("(kc p) o -> p kc o", p=128)
    pr.dma("gpsimd", wo_sb[:, :].rearrange("p (kc o) -> p kc o", kc=8), wo_r,
           writes=[B["wo"]])
    pr.dma("gpsimd", rb2_sb[:, :], rb2, writes=[B["rb2"]])

    emit_cc(HG - 1)

    pr.add("vector", lambda v: v.memset(onesb_sb[:, :], 1.0),
           writes=[B["onesb"]])

    def emit_lq(k):
        rows = cc_out[k:k + 1].rearrange("a (hl r) t -> (a hl) r t", r=65)
        half = OWN // 2
        for ch in range(2):
            pr.dma("sync", lpf_sb[:, :], rows[:, 64, ch * half:(ch + 1) * half],
                   reads=[B[f"ccout{k}"]], writes=[B["lpf"]])
            pr.add("scalar", lambda s: s.activation(lpf_sb[:, :], lpf_sb[:, :],
                   AF.Ln), reads=[B["lpf"]], writes=[B["lpf"]])
            pr.add("scalar", lambda s, ch=ch: s.activation(
                lpb_sb[:, ch * half:(ch + 1) * half], lpf_sb[:, :],
                AF.Exp, scale=-1.0), reads=[B["lpf"]], writes=[B["lpb"]])

    def emit_rhead(h):
            hh, hl = divmod(h, 4)
            pr.dma("sync", rstg_sb[:, :],
                   cc_out[hh, hl * 65:hl * 65 + 64, :],
                   reads=[B[f"ccout{hh}"]], writes=[B["rstg"]])
            pr.dma("sync", lr_sb[64:65, :], lpb_sb[hl:hl + 1, :],
                   reads=[B["lpb"]], writes=[B["lr"]])

    def emit_readback(h, qt):
            pbase = 64 * (h % 2)
            h2 = h // 2
            par = (h * NQO + qt) % 2
            bc = psA2[:, par * 512:(par + 1) * 512]
            bcB = B["p2A0"] if par == 0 else B["p2A1"]
            pr.add("tensor", lambda t, bc=bc, qt=qt: t.matmul(
                bc[0:64, :], onesb_sb[64:65, :],
                lr_sb[64:65, qt * 512: qt * 512 + 512],
                start=True, stop=True), reads=[B["onesb"], B["lr"]],
                writes=[bcB])
            dst = ctx_sb[pbase:pbase + 64, h2 * OWN + qt * 512:
                         h2 * OWN + qt * 512 + 512]
            pr.add("vector", lambda v, dst=dst, qt=qt, bc=bc:
                v.tensor_tensor(dst, rstg_sb[0:64, qt * 512: qt * 512 + 512],
                                bc[0:64, :], op=ALU.mult),
                reads=[B["rstg"], bcB], writes=[B["ctx"]])

    rb_work = [(h, qt) for h in range(NH) for qt in range(NQO)]
    rb_pos = [0]
    lq_done = [False] * HG

    def drain_readback(n):
        while rb_pos[0] < len(rb_work) and n > 0:
            h, qt = rb_work[rb_pos[0]]
            if not lq_done[h // 4]:
                emit_lq(h // 4)
                lq_done[h // 4] = True
            if qt == 0:
                emit_rhead(h)
            emit_readback(h, qt)
            rb_pos[0] += 1
            n -= 1

    x_own_t = x_own.rearrange("(tc p) d -> tc p d", p=128)

    def emit_gate(tc):
        gp = tc % 2
        i = tc % 2
        pg = (psC2 if i == 0 else psD2)[:, 0:NE]
        pgB = B["p2C"] if i == 0 else B["p2D"]
        emits = []
        for kc in range(8):
            emits.append(
                lambda t, pg=pg,
                l=xnofm_sb[:, kc * OWN + tc * 128: kc * OWN + tc * 128 + 128],
                r=gw_sb[:, kc * NE: kc * NE + NE],
                s=(kc == 0), e=(kc == 7):
                t.matmul(pg, l, r, start=s, stop=e))
        pr.tgroup(emits, reads=[B["xnofm"], B["gw"]], writes=[pgB])
        W = gwork_sb[:, gp * 112: gp * 112 + 112]
        gx = W[:, 0:NE]; ge = W[:, 16:16 + NE]; g = W[:, 32:32 + NE]
        g1 = W[:, 48:48 + NE]; g2 = W[:, 64:64 + NE]; tmp = W[:, 80:80 + NE]
        gated = W[:, 96:96 + NE]
        S = gsc_sb[:, gp * 16: gp * 16 + 16]
        mx = S[:, 0:1]; nmx = S[:, 1:2]; sm = S[:, 2:3]; rsm = S[:, 3:4]
        m1 = S[:, 4:5]; m2 = S[:, 5:6]; m3 = S[:, 6:7]
        gwB, gsB = B[f"gwork{gp}"], B[f"gsc{gp}"]
        gpad = gpad_sb[:, gp * 128: gp * 128 + 128]
        gpB = B[f"gpad{gp}"]

        def va(fn, reads, writes):
            pr.add("gpsimd", fn, reads=reads, writes=writes)

        pr.add("vector", lambda v, pg=pg, gx=gx: v.tensor_tensor(
            gx, pg, gb_sb[:, :], op=ALU.add), reads=[pgB, B["gb"]], writes=[gwB])
        pr.add("vector", lambda v, gx=gx, mx=mx: v.reduce_max(mx, gx, axis=AX.X), reads=[gwB], writes=[gsB])
        va(lambda v, mx=mx, nmx=nmx: v.tensor_scalar_mul(nmx, mx, -1.0), [gsB], [gsB])
        pr.add("scalar", lambda s, gx=gx, ge=ge, nmx=nmx: s.activation(
            ge, gx, AF.Exp, bias=nmx), reads=[gwB, gsB], writes=[gwB])
        pr.add("vector", lambda v, ge=ge, sm=sm: v.reduce_sum(sm, ge, axis=AX.X), reads=[gwB], writes=[gsB])
        pr.add("vector", lambda v, sm=sm, rsm=rsm: v.reciprocal(rsm, sm),
               reads=[gsB], writes=[gsB])
        va(lambda v, ge=ge, g=g, rsm=rsm: v.tensor_scalar_mul(g, ge, rsm),
           [gwB, gsB], [gwB])
        pr.add("vector", lambda v, g=g, m1=m1: v.reduce_max(m1, g, axis=AX.X), reads=[gwB], writes=[gsB])
        va(lambda v, g=g, m1=m1, tmp=tmp: v.tensor_scalar(
            tmp, g, m1, None, op0=ALU.is_ge), [gwB, gsB], [gwB])
        va(lambda v, g=g, tmp=tmp: v.tensor_tensor(tmp, tmp, g, op=ALU.mult),
           [gwB], [gwB])
        va(lambda v, g=g, g1=g1, tmp=tmp: v.tensor_tensor(g1, g, tmp, op=ALU.subtract),
           [gwB], [gwB])
        pr.add("vector", lambda v, g1=g1, m2=m2: v.reduce_max(m2, g1, axis=AX.X), reads=[gwB], writes=[gsB])
        va(lambda v, g1=g1, m2=m2, tmp=tmp: v.tensor_scalar(
            tmp, g1, m2, None, op0=ALU.is_ge), [gwB, gsB], [gwB])
        va(lambda v, g1=g1, tmp=tmp: v.tensor_tensor(tmp, tmp, g1, op=ALU.mult),
           [gwB], [gwB])
        va(lambda v, g1=g1, g2=g2, tmp=tmp: v.tensor_tensor(g2, g1, tmp, op=ALU.subtract),
           [gwB], [gwB])
        pr.add("vector", lambda v, g2=g2, m3=m3: v.reduce_max(m3, g2, axis=AX.X), reads=[gwB], writes=[gsB])
        va(lambda v, g=g, m3=m3, tmp=tmp: v.tensor_scalar(
            tmp, g, m3, None, op0=ALU.is_ge), [gwB, gsB], [gwB])
        va(lambda v, g=g, tmp=tmp, gated=gated: v.tensor_tensor(
            gated, g, tmp, op=ALU.mult), [gwB], [gwB])
        va(lambda v, gated=gated, tc=tc: v.tensor_copy(
            gated_tm_sb[:, tc * NE: tc * NE + NE], gated), [gwB], [B["gated_tm"]])
        va(lambda v, gated=gated, gpad=gpad: v.tensor_copy(gpad[:, 0:NE], gated),
           [gwB], [gpB])
        pr.add("tensor", lambda t, gpad=gpad: t.transpose(
            psG2[:, 0:128], gpad, identb_sb[:, :]),
            reads=[gpB, B["identb"]], writes=[B["p2E"]])
        pr.add("vector", lambda v, tc=tc: v.tensor_copy(
            gfm_sb[0:NE, tc * 128:(tc + 1) * 128], psG2[0:NE, 0:128]),
            reads=[B["p2E"]], writes=[B["gfm"]])

    pr.add("vector", lambda v: v.memset(gpad_sb[:, :], 0.0),
           writes=[B["gpad0"], B["gpad1"]])
    for it in range(NTC_OWN + 1):
        if it >= 1:
            tc = it - 1
            par = tc % 2
            xnts, xntB = emit_ln_b(tc, "2", xt2_sb, xnt2_sb, rstd2_sb, negmr2_sb,
                                   depth=2, mv_t=mv2_sb)
        if it < NTC_OWN:
            emit_ln_a(it, x_own_t, "2", xt2_sb, stat2_sb, mv2_sb, rstd2_sb,
                      negmr2_sb, depth=2)
        if it < 1:
            continue
        for j in range(8):
            pb = B["p2F"] if j % 2 == 0 else B["p2D"]
            pt = (psH2 if j % 2 == 0 else psD2)[:, 0:128]
            pr.add("tensor", lambda t, pt=pt, src=xnts[:, j * 128:(j + 1) * 128]:
                   t.transpose(pt, src, identf_sb[:, :]),
                   reads=[xntB, B["identf"]], writes=[pb])
            dst = acc_sb[:, j * OWN + tc * 128: j * OWN + tc * 128 + 128]
            pr.add("vector", lambda v, pt=pt, dst=dst: v.tensor_tensor(
                dst, dst, pt, op=ALU.add), reads=[pb, B["acc"]], writes=[B["acc"]])
            dst2 = xnofm_sb[:, j * OWN + tc * 128: j * OWN + tc * 128 + 128]
            pr.add("vector", lambda v, pt=pt, dst2=dst2: v.tensor_copy(dst2, pt),
                   reads=[pb], writes=[B["xnofm"]])
        xnbf = xnbf_sb[:, par * 1024:(par + 1) * 1024]
        xnbB = B[f"xnbf{par}"]
        pr.add("gpsimd", lambda g, xnts=xnts, xnbf=xnbf: g.tensor_copy(xnbf, xnts),
               reads=[xntB], writes=[xnbB])
        pr.dma("sync", xn_dram[:, :, tc * 128:(tc + 1) * 128].rearrange(
                   "dc p q -> p dc q"),
               xnbf.rearrange("p (dc q) -> p dc q", dc=8),
               reads=[xnbB], writes=[B["xndram"]])
        emit_gate(tc)
        drain_readback(4)

    drain_readback(len(rb_work))

    for oc in range(8):
        pr.add("vector", lambda v, oc=oc: v.tensor_scalar(
            acc_sb[:, oc * OWN:(oc + 1) * OWN], acc_sb[:, oc * OWN:(oc + 1) * OWN],
            bo_sb[:, oc:oc + 1], sb2_sb[:, oc:oc + 1], op0=ALU.add, op1=ALU.add),
            reads=[B["acc"], B["bo"], B["sb2"]], writes=[B["acc"]])


    pr.add("vector", lambda v: v.tensor_scalar(
        mfm_sb[:, :], gfm_sb[0:NE, :], 0.0, None, op0=ALU.is_gt),
        reads=[B["gfm"]], writes=[B["mfm"]])
    pr.add("vector", lambda v: v.tensor_tensor_scan(
        pm_sb[:, :], mfm_sb[:, :], mfm_sb[:, :], 0.0,
        op0=ALU.add, op1=ALU.bypass), reads=[B["mfm"]], writes=[B["PM"]])
    pr.add("vector", lambda v: v.tensor_tensor(
        pm_sb[:, :], pm_sb[:, :], mfm_sb[:, :], op=ALU.mult),
        reads=[B["PM"], B["mfm"]], writes=[B["PM"]])
    pr.add("vector", lambda v: v.tensor_scalar_add(pm_sb[:, :], pm_sb[:, :], -1.0),
           reads=[B["PM"]], writes=[B["PM"]])
    for tc in range(NTC_OWN):
        pr.add("tensor", lambda t, tc=tc: t.transpose(
            psH2[:, 0:NE], pm_sb[0:NE, tc * 128:(tc + 1) * 128],
            identf_sb[0:NE, 0:NE]),
            reads=[B["PM"], B["identf"]], writes=[B["p2F"]])
        pr.add("vector", lambda v, tc=tc: v.tensor_copy(
            pos_tm_sb[:, tc * NE: tc * NE + NE], psH2[:, 0:NE]),
            reads=[B["p2F"]], writes=[B["pos_tm"]])

    for oc in range(8):
        for qt in range(NTT):
            i = (oc * NTT + qt) % 2
            pv = psA2[:, i * 512: i * 512 + 512]
            pvB = B["p2A0"] if i == 0 else B["p2A1"]
            emits = []
            for kc in range(8):
                emits.append(
                    lambda t, pv=pv,
                    l=wo_sb[:, kc * 1024 + oc * 128: kc * 1024 + oc * 128 + 128],
                    r=ctx_sb[:, kc * OWN + qt * 512: kc * OWN + qt * 512 + 512],
                    s=(kc == 0), e=(kc == 7):
                    t.matmul(pv, l, r, start=s, stop=e))
            pr.tgroup(emits, reads=[B["wo"], B["ctx"]], writes=[pvB])
            dst = acc_sb[:, oc * OWN + qt * 512: oc * OWN + qt * 512 + 512]
            pr.add("vector", lambda v, pv=pv, dst=dst: v.scalar_tensor_tensor(
                dst, pv, 1.0, dst, op0=ALU.mult, op1=ALU.add),
                reads=[pvB, B["acc"]], writes=[B["acc"]])

    for oc in range(8):
        for qt in range(NTT):
            i = (oc * NTT + qt) % 2
            pv = psB2[:, i * 512: i * 512 + 512]
            pvB = B["p2B0"] if i == 0 else B["p2B1"]
            pr.add("tensor", lambda t, pv=pv,
                   l=rb2_sb[0:NE, oc * 128:(oc + 1) * 128],
                   r=gfm_sb[0:NE, qt * 512: qt * 512 + 512]:
                   t.matmul(pv, l, r, start=True, stop=True),
                   reads=[B["rb2"], B["gfm"]], writes=[pvB])
            dst = acc_sb[:, oc * OWN + qt * 512: oc * OWN + qt * 512 + 512]
            pr.add("vector", lambda v, pv=pv, dst=dst: v.scalar_tensor_tensor(
                dst, pv, 1.0, dst, op0=ALU.mult, op1=ALU.add),
                reads=[pvB, B["acc"]], writes=[B["acc"]])

    es_s2b.close()

    pr.new_seg()

    w1s_sb = sb(es_s2w, "w1s_sb", [128, 8 * 2048], BF16)
    w2s_sb = sb(es_s2w, "w2s_sb", [128, 16 * 1024], BF16)
    psA4 = ps(es_s2w, "psA4", [128, 1024])
    psB4 = ps(es_s2w, "psB4", [128, 1024])
    nb("w1s", "w2s", "p4A0", "p4A1", "p4B0", "p4B1", "hS")
    h_sb = pool_sb

    sw1_r = sw1.rearrange("(kc p) f -> p kc f", p=128)
    sw2_r = sw2.rearrange("(fc p) o -> p fc o", p=128)

    for half in range(2):
        src1 = sw1_r[:, :, half * 2048:(half + 1) * 2048]
        src2 = sw2_r[:, half * 16:(half + 1) * 16, :]
        pr.dma("gpsimd", w1s_sb[:, :].rearrange("p (kc f) -> p kc f", kc=8), src1,
               writes=[B["w1s"]])
        pr.dma("gpsimd", w2s_sb[:, :].rearrange("p (fc o) -> p fc o", fc=16), src2,
               writes=[B["w2s"]])
        for tt in range(NTT):
            for fc in range(16):
                i = fc % 2
                ph = psA4[:, i * 512: i * 512 + 512]
                phB = B["p4A0"] if i == 0 else B["p4A1"]
                emits = []
                for kc in range(8):
                    emits.append(
                        lambda t, ph=ph,
                        l=w1s_sb[:, kc * 2048 + fc * 128: kc * 2048 + fc * 128 + 128],
                        r=xnofm_sb[:, kc * OWN + tt * 512: kc * OWN + tt * 512 + 512],
                        s=(kc == 0), ee=(kc == 7):
                        t.matmul(ph, l, r, start=s, stop=ee))
                pr.tgroup(emits, reads=[B["w1s"], B["xnofm"]], writes=[phB])
                b1col = sb1_sb[:, half * 16 + fc: half * 16 + fc + 1]
                hdst = h_sb[:, fc * 512:(fc + 1) * 512]
                pr.add("scalar", lambda s, ph=ph, hdst=hdst, b1col=b1col:
                       s.activation(hdst, ph, AF.Gelu, bias=b1col),
                       reads=[phB, B["sb1"]], writes=[B["hS"]])
            for oc in range(8):
                i = oc % 2
                pv = psB4[:, i * 512: i * 512 + 512]
                pvB = B["p4B0"] if i == 0 else B["p4B1"]
                emits = []
                for fc in range(16):
                    emits.append(
                        lambda t, pv=pv,
                        l=w2s_sb[:, fc * 1024 + oc * 128: fc * 1024 + oc * 128 + 128],
                        r=h_sb[:, fc * 512:(fc + 1) * 512],
                        s=(fc == 0), ee=(fc == 15):
                        t.matmul(pv, l, r, start=s, stop=ee))
                pr.tgroup(emits, reads=[B["w2s"], B["hS"]], writes=[pvB])
                dst = acc_sb[:, oc * OWN + tt * 512: oc * OWN + tt * 512 + 512]
                pr.add("vector", lambda v, pv=pv, dst=dst: v.scalar_tensor_tensor(
                    dst, pv, 1.0, dst, op0=ALU.mult, op1=ALU.add),
                    reads=[pvB, B["acc"]], writes=[B["acc"]])

    es_s2w.close()
    es_s2a.close()

    pr.new_seg()

    xn_tm_sb = sb(es_s3, "xn_tm_sb", [128, 2 * (NTC_OWN * 128)], BF16)
    pet_sb = sb(es_s3, "pet_sb", [128, NTC_OWN * 512], BF16)
    pe_sb = sb(es_s3, "pe_sb", [128, SC * 512], BF16)
    gv_sb = sb(es_s3, "gv_sb", [128, 4])
    iota_sb = sb(es_s3, "iota_sb", [128, 512], mybir.dt.int16)
    sid_sb = sb(es_s3, "sid_sb", [128, 4])
    onehf_sb = sb(es_s3, "onehf_sb", [NE, NE * 128])
    w1_sb = sb(es_s3, "w1_sb", [128, 8 * 2048], BF16)
    w2_sb = sb(es_s3, "w2_sb", [128, 16 * 1024], BF16)
    x_sb = pool_sb[:, 8192: 8192 + 8 * 512]
    osm_sb = pool_sb[:, 12288: 12288 + SC * 1024]

    psA3 = ps(es_s3, "psA3", [128, 1024])
    psB3 = ps(es_s3, "psB3", [128, 1024])
    psC3 = ps(es_s3, "psC3", [128, 512])
    psD3 = ps(es_s3, "psD3", [128, 512])

    nb(*[f"pet{i}" for i in range(NTC_OWN)])
    nb("pex")
    nb("xntm0", "xntm1", "pe", "gv", "iota", "sid", "onehf",
       "w1", "w2", "h", "x", "osm",
       "p3A0", "p3A1", "p3B0", "p3B1", "p3C", "p3D")

    pr.dma("sync", iota_sb[:, :], slotiota, writes=[B["iota"]])
    pr.dma("sync", sid_sb[:, :], slotid, writes=[B["sid"]])
    pr.dma("sync", onehf_sb[:, :], onehf, writes=[B["onehf"]])

    rw1_r = rw1.rearrange("e (kc p) f -> e p kc f", p=128)
    rw2_r = rw2.rearrange("e (fc p) o -> e p fc o", p=128)

    DCW = NTC_OWN * 128

    def emit_pet(e):
        for tc in range(NTC_OWN):
            pr.add("vector", lambda v, tc=tc, e=e: v.tensor_scalar(
                pet_sb[:, tc * 512:(tc + 1) * 512], iota_sb[:, :],
                pos_tm_sb[:, tc * NE + e: tc * NE + e + 1], None, op0=ALU.is_equal),
                reads=[B["iota"], B["pos_tm"]], writes=[B[f"pet{tc}"]])

    emit_pet(0)
    for e in range(NE):
        pr.dma("gpsimd", w1_sb[:, :].rearrange("p (kc f) -> p kc f", kc=8),
               rw1_r[e], writes=[B["w1"]])
        pr.dma("gpsimd", w2_sb[:, :].rearrange("p (fc o) -> p fc o", fc=16),
               rw2_r[e], writes=[B["w2"]])
        for sc in range(SC):
            emits = []
            for tc in range(NTC_OWN):
                emits.append(
                    lambda t, sc=sc, tc=tc, e=e:
                    t.matmul(psB3[:, sc:sc + 1],
                             pet_sb[:, tc * 512 + sc * 128: tc * 512 + sc * 128 + 128],
                             gated_tm_sb[:, tc * NE + e: tc * NE + e + 1],
                             start=(tc == 0), stop=(tc == NTC_OWN - 1)))
            pr.tgroup(emits, reads=[B[f"pet{tc}"] for tc in range(NTC_OWN)]
                      + [B["gated_tm"]], writes=[B["p3B0"]])
        pr.add("vector", lambda v: v.tensor_copy(gv_sb[:, 0:SC], psB3[:, 0:SC]),
               reads=[B["p3B0"]], writes=[B["gv"]])
        for dc in range(8):
            sl = dc % 2
            xnsl = xn_tm_sb[:, sl * DCW:(sl + 1) * DCW]
            xnB = B[f"xntm{sl}"]
            pr.dma("sync", xnsl, xn_dram[dc],
                   reads=[B["xndram"]], writes=[xnB])
            i = dc % 2
            pg = (psC3 if i == 0 else psD3)[:, 0:512]
            pgB = B["p3C"] if i == 0 else B["p3D"]
            emits = []
            for tc in range(NTC_OWN):
                emits.append(
                    lambda t, pg=pg, xnsl=xnsl, tc=tc:
                    t.matmul(pg, xnsl[:, tc * 128:(tc + 1) * 128],
                             pet_sb[:, tc * 512:(tc + 1) * 512],
                             start=(tc == 0), stop=(tc == NTC_OWN - 1)))
            pr.tgroup(emits, reads=[xnB] + [B[f"pet{tc}"] for tc in range(NTC_OWN)],
                      writes=[pgB])
            pexdeps = list(B["pex"].rs.values()) + (
                [B["pex"].w] if B["pex"].w is not None else [])
            pr.add("vector", lambda v, pg=pg, dc=dc: v.tensor_copy(
                x_sb[:, dc * 512:(dc + 1) * 512], pg),
                reads=[pgB], writes=[B["x"]], deps=pexdeps)
        if e + 1 < NE:
            emit_pet(e + 1)
        for fc in range(16):
            i = fc % 2
            ph = (psC3 if i == 0 else psD3)[:, 0:512]
            phB = B["p3C"] if i == 0 else B["p3D"]
            emits = []
            for kc in range(8):
                emits.append(
                    lambda t, ph=ph,
                    l=w1_sb[:, kc * 2048 + fc * 128: kc * 2048 + fc * 128 + 128],
                    r=x_sb[:, kc * 512:(kc + 1) * 512],
                    s=(kc == 0), ee=(kc == 7):
                    t.matmul(ph, l, r, start=s, stop=ee))
            pr.tgroup(emits, reads=[B["w1"], B["x"]], writes=[phB])
            b1col = rb1_sb[:, e * 16 + fc: e * 16 + fc + 1]
            hdst = h_sb[:, fc * 512:(fc + 1) * 512]
            pr.add("scalar", lambda s, ph=ph, hdst=hdst, b1col=b1col:
                   s.activation(hdst, ph, AF.Gelu, bias=b1col),
                   reads=[phB, B["rb1"]], writes=[B["h"]])
        for sc in range(SC):
            for dh in range(2):
                i = (sc * 2 + dh) % 2
                pv = (psC3 if i == 0 else psD3)[:, 0:512]
                pvB = B["p3C"] if i == 0 else B["p3D"]
                emits = []
                for fc in range(16):
                    emits.append(
                        lambda t, pv=pv,
                        l=h_sb[:, fc * 512 + sc * 128: fc * 512 + sc * 128 + 128],
                        r=w2_sb[:, fc * 1024 + dh * 512: fc * 1024 + dh * 512 + 512],
                        s=(fc == 0), ee=(fc == 15):
                        t.matmul(pv, l, r, start=s, stop=ee))
                pr.tgroup(emits, reads=[B["h"], B["w2"]], writes=[pvB])
                odst = osm_sb[:, sc * 1024 + dh * 512: sc * 1024 + dh * 512 + 512]
                pr.add("scalar", lambda s, pv=pv, odst=odst, sc=sc: s.activation(
                    odst, pv, AF.Identity, scale=gv_sb[:, sc:sc + 1]),
                    reads=[pvB, B["gv"]], writes=[B["osm"]])
        pe_alt = x_sb
        def build_pe(tt, e=e):
            par_ = tt % 2
            slot = pe_sb if par_ == 0 else pe_alt
            peB = B["pe"] if par_ == 0 else B["pex"]
            pb = psB3[:, 512:1024] if par_ == 0 else psB3[:, 0:512]
            pbB = B["p3B1"] if par_ == 0 else B["p3B0"]
            extra = []
            if par_ == 1:
                xb = B["x"]
                extra = list(xb.rs.values()) + ([xb.w] if xb.w is not None else [])
            pr.add("tensor", lambda t, e=e, tt=tt, pb=pb: t.matmul(
                pb, onehf_sb[0:NE, e * 128:(e + 1) * 128],
                pm_sb[0:NE, tt * 512: tt * 512 + 512], start=True, stop=True),
                reads=[B["onehf"], B["PM"]], writes=[pbB])
            for sc in range(SC):
                pr.add("vector", lambda v, sc=sc, slot=slot, pb=pb: v.tensor_scalar(
                    slot[:, sc * 512:(sc + 1) * 512], pb,
                    sid_sb[:, sc:sc + 1], None, op0=ALU.is_equal),
                    reads=[pbB, B["sid"]], writes=[peB], deps=extra)

        def scatter_tt(tt):
            par_ = tt % 2
            slot = pe_sb if par_ == 0 else pe_alt
            peB = B["pe"] if par_ == 0 else B["pex"]
            for dc in range(8):
                i = dc % 2
                pv = psA3[:, i * 512: i * 512 + 512]
                pvB = B["p3A0"] if i == 0 else B["p3A1"]
                emits = []
                for sc in range(SC):
                    emits.append(
                        lambda t, pv=pv,
                        l=osm_sb[:, sc * 1024 + dc * 128: sc * 1024 + dc * 128 + 128],
                        r=slot[:, sc * 512:(sc + 1) * 512],
                        s=(sc == 0), ee=(sc == SC - 1):
                        t.matmul(pv, l, r, start=s, stop=ee))
                pr.tgroup(emits, reads=[B["osm"], peB], writes=[pvB])
                dst = acc_sb[:, dc * OWN + tt * 512: dc * OWN + tt * 512 + 512]
                pr.add("vector", lambda v, pv=pv, dst=dst: v.scalar_tensor_tensor(
                    dst, pv, 1.0, dst, op0=ALU.mult, op1=ALU.add),
                    reads=[pvB, B["acc"]], writes=[B["acc"]])

        build_pe(0)
        for tt in range(NTT):
            if tt + 1 < NTT:
                build_pe(tt + 1)
            scatter_tt(tt)

    out_r = out_fm.rearrange("(oc p) t -> p oc t", p=128)
    pr.dma("sync", out_r, acc_sb[:, :].rearrange("p (oc t) -> p oc t", oc=8),
           reads=[B["acc"]])

    totals = pr.finalize_and_emit(nc, sems)
    pr.emit_blocks(nc, sems)

    with nc.Block("fin") as block:
        def fin(s):
            s.wait_ge(sems["sync"], totals["sync"])
            s.wait_ge(sems["gpsimd"], totals["gpsimd"])
        block.sync(fin)

    es_s3.close()
    es_acc.close()
    es_ctx.close()
    es_global.close()
    es_sem.close()
    return nc



def host_prep(x, wq, wk, wv, wo, b_o, sw1, sb1, sw2, sb2, rw1, rb1, rw2,
              rb2, gw, gb, cfg):
    T, NH, NE = cfg["T"], cfg["NH"], cfg["NE"]
    OWN = T // 2
    bf = ml_dtypes.bfloat16
    f32 = np.float32

    x = np.ascontiguousarray(np.asarray(x, dtype=f32))
    nseq = x.shape[0]

    def c32(a):
        return np.ascontiguousarray(np.asarray(a, dtype=f32))

    wq, wk, wv, wo = c32(wq), c32(wk), c32(wv), c32(wo)
    sw1, sw2 = c32(sw1).reshape(1024, 4096), c32(sw2).reshape(4096, 1024)
    rw1, rw2 = c32(rw1), c32(rw2)
    rb2a = c32(rb2).reshape(NE, 1024)
    gwa = c32(gw)
    gba = c32(gb).reshape(NE)
    b_o = c32(b_o).reshape(1024)
    sb1a = c32(sb1).reshape(4096)
    sb2a = c32(sb2).reshape(1024)
    rb1a = c32(rb1).reshape(NE, 2048)

    gw_pp = np.ascontiguousarray(
        gwa.reshape(8, 128, NE).transpose(1, 0, 2).reshape(128, 8 * NE))
    gb_bc = np.ascontiguousarray(np.broadcast_to(gba[None, :], (128, NE)))
    bo_pp = np.ascontiguousarray(b_o.reshape(8, 128).T)
    sb2_pp = np.ascontiguousarray(sb2a.reshape(8, 128).T)
    sb1_pp = np.ascontiguousarray(sb1a.reshape(32, 128).T)
    rb1_pp = np.ascontiguousarray(
        rb1a.reshape(NE, 16, 128).transpose(2, 0, 1).reshape(128, NE * 16))

    kk = np.arange(128)[:, None]
    jj = np.arange(512)[None, :]
    identb = np.eye(128, dtype=bf)
    identf = np.eye(128, dtype=f32)
    onehf = np.zeros((NE, NE * 128), dtype=f32)
    for e in range(NE):
        onehf[e, e * 128:(e + 1) * 128] = 1.0
    slotiota = np.tile(np.arange(512, dtype=np.int16), (128, 1))
    slotid = (np.arange(4, dtype=f32)[None, :] * 128
              + np.arange(128, dtype=f32)[:, None])
    slotid = np.ascontiguousarray(slotid)

    shared = dict(wq=wq, wk=wk, wv=wv, wo=wo, rb2=rb2a, gw_pp=gw_pp,
                  gb_bc=gb_bc, bo_pp=bo_pp, sb2_pp=sb2_pp, sb1_pp=sb1_pp,
                  rb1_pp=rb1_pp, sw1=sw1, sw2=sw2, rw1=rw1, rw2=rw2,
                  identb=identb, identf=identf, onehf=onehf,
                  slotiota=slotiota, slotid=slotid)

    in_maps = []
    for c in range(2 * nseq):
        s, r = c // 2, c % 2
        xo = np.ascontiguousarray(x[s, r * OWN:(r + 1) * OWN])
        xkv = np.ascontiguousarray(
            x[s].reshape(T // 128, 128, 1024)[r::2].reshape(OWN, 1024))
        m = np.concatenate(
            [np.tile(np.where(kk + 128 * (2 * i + r) > jj, 0.0, 1.0), (1, 2))
             for i in range(2)], axis=1)
        m_ = dict(shared)
        m_.update(x_all=x[s], x_own=xo, x_kv=xkv,
                  x_own_fm=np.ascontiguousarray(xo.T),
                  masks=m.astype(bf))
        in_maps.append(m_)
    return in_maps


def kernel(L0_attn_x, L0_attn_W_q, L0_attn_W_k, L0_attn_W_v, L0_attn_W_o,
           L0_attn_b_o, L0_ffn_sW1, L0_ffn_sb1, L0_ffn_sW2, L0_ffn_sb2,
           L0_ffn_rW1, L0_ffn_rb1, L0_ffn_rW2, L0_ffn_rb2, L0_ffn_gW,
           L0_ffn_gb):
    cfg = full_cfg()
    in_maps = host_prep(L0_attn_x, L0_attn_W_q, L0_attn_W_k, L0_attn_W_v,
                        L0_attn_W_o, L0_attn_b_o, L0_ffn_sW1, L0_ffn_sb1,
                        L0_ffn_sW2, L0_ffn_sb2, L0_ffn_rW1, L0_ffn_rb1,
                        L0_ffn_rW2, L0_ffn_rb2, L0_ffn_gW, L0_ffn_gb, cfg)
    nc = build_nc(cfg)
    res = run_bass_kernel_spmd(nc, in_maps, core_ids=list(range(8)))
    OWN = cfg["T"] // 2
    out = np.empty((4, cfg["T"], 1024), dtype=np.float32)
    for c in range(8):
        s, r = c // 2, c % 2
        out[s, r * OWN:(r + 1) * OWN] = np.asarray(res.results[c]["out_fm"]).T
    return out


# version 43
# speedup vs baseline: 1.0906x; 1.0192x over previous
import numpy as np
import ml_dtypes
from contextlib import ExitStack

import concourse.bass as bass
import concourse.mybir as mybir
from concourse.bass_utils import run_bass_kernel_spmd

F32 = mybir.dt.float32
BF16 = mybir.dt.bfloat16
AF = mybir.ActivationFunctionType
ALU = mybir.AluOpType
AX = mybir.AxisListType

ENGINES = ("tensor", "vector", "scalar", "gpsimd", "sync")
COUNTERS = ENGINES + ("cc", "pool", "sdma")
MASKVAL = -100000.0


def full_cfg():
    return dict(T=4096, NH=16, NE=15)



class Op:
    __slots__ = ("eng", "emit", "deps", "dma", "cc", "tok", "need")

    def __init__(self, eng, emit, deps, dma, cc=False):
        self.eng, self.emit, self.deps, self.dma = eng, emit, deps, dma
        self.cc = cc
        self.tok, self.need = None, False


class Buf:
    __slots__ = ("w", "rs")

    def __init__(self):
        self.w = None
        self.rs = {}


class Prog:
    def __init__(self):
        self.segs = [[]]
        self.ops = []

    def new_seg(self):
        self.segs.append([])

    def _deps_for(self, reads, writes, deps):
        dd = [d for d in deps if d is not None]
        for b in reads:
            if b.w is not None:
                dd.append(b.w)
        for b in writes:
            if b.w is not None:
                dd.append(b.w)
            dd.extend(b.rs.values())
        return dd

    def _register(self, op_first, op_last, reads, writes):
        for b in reads:
            b.rs[op_last.eng] = op_last
        for b in writes:
            b.w = op_last
            b.rs = {}

    def add(self, eng, emit, reads=(), writes=(), deps=(), dma=False, cc=False):
        o = Op(eng, emit, self._deps_for(reads, writes, deps), dma, cc)
        self.segs[-1].append(o)
        self.ops.append(o)
        self._register(o, o, reads, writes)
        return o

    def tgroup(self, emits, reads=(), writes=(), deps=()):
        first = Op("tensor", emits[0], self._deps_for(reads, writes, deps), False)
        self.segs[-1].append(first)
        self.ops.append(first)
        last = first
        for e in emits[1:]:
            last = Op("tensor", e, [], False)
            self.segs[-1].append(last)
            self.ops.append(last)
        self._register(first, last, reads, writes)
        return last

    def dma(self, eng, out, in_, reads=(), writes=(), deps=()):
        def emit(e, out=out, in_=in_):
            return e.dma_start(out=out, in_=in_)
        return self.add(eng, emit, reads=reads, writes=writes, deps=deps, dma=True)

    def finalize_and_emit(self, nc, sems):
        for o in self.ops:
            for d in o.deps:
                d.need = True
            if o.dma:
                o.need = True
        for seg in self.segs:
            last = {}
            for o in seg:
                last[o.eng] = o
            for o in last.values():
                o.need = True
        cnt = {e: 0 for e in COUNTERS}
        seg_end = []
        for seg in self.segs:
            for o in seg:
                if o.need:
                    if o.cc:
                        key = "cc"
                    elif o.eng == "gpsimd" and not o.dma:
                        key = "pool"
                    elif o.eng == "scalar" and o.dma:
                        key = "sdma"
                    else:
                        key = o.eng
                    cnt[key] += 16 if o.dma else 1
                    o.tok = (key, cnt[key])
            seg_end.append(dict(cnt))

        self.seg_end = seg_end
        return seg_end[-1]

    def emit_blocks(self, nc, sems):
        seg_end = self.seg_end
        water = {e: {f: 0 for f in COUNTERS} for e in ENGINES}
        for si, seg in enumerate(self.segs):
            eng_ops = {e: [o for o in seg if o.eng == e] for e in ENGINES}
            if not any(eng_ops.values()):
                continue
            with nc.Block(f"seg{si}") as block:
                def mk_body(e):
                    def body(eng, e=e, si=si):
                        if si > 0:
                            prev = seg_end[si - 1]
                            for f in COUNTERS:
                                if f != e and prev[f] > water[e][f]:
                                    eng.wait_ge(sems[f], prev[f])
                                    water[e][f] = prev[f]
                        for o in eng_ops[e]:
                            req = {}
                            if o.dma and o.tok is not None:
                                req[o.tok[0]] = o.tok[1] - 16
                            for d in o.deps:
                                if d.tok is not None:
                                    fe, v = d.tok
                                    if v > req.get(fe, 0):
                                        req[fe] = v
                            for fe, v in req.items():
                                if v > water[e][fe]:
                                    eng.wait_ge(sems[fe], v)
                                    water[e][fe] = v
                            inst = o.emit(eng)
                            if o.tok is not None:
                                inst.then_inc(sems[o.tok[0]], 16 if o.dma else 1)
                    return body
                for e in ENGINES:
                    getattr(block, e)(mk_body(e))



def build_nc(cfg):
    T = cfg["T"]
    NH = cfg["NH"]
    NE = cfg["NE"]
    OWN = T // 2
    HG = NH // 4
    NTC_ALL = T // 128
    NTC_OWN = OWN // 128
    NQT = T // 512
    NQO = OWN // 512
    NHH = NH // 2
    NTT = OWN // 512
    CAP = 512
    SC = CAP // 128

    nc = bass.Bass("TRN2", target_bir_lowering=False, debug=False)

    def din(name, shape, dt=F32):
        return nc.dram_tensor(name, shape, dt, kind="ExternalInput").ap()

    x_all = din("x_all", [T, 1024])
    x_own = din("x_own", [OWN, 1024])
    x_own_fm = din("x_own_fm", [1024, OWN])
    x_kv = din("x_kv", [OWN, 1024])
    wq = din("wq", [1024, 1024])
    wk = din("wk", [1024, 1024])
    wv = din("wv", [1024, 1024])
    wo = din("wo", [1024, 1024])
    rb2 = din("rb2", [NE, 1024])
    gw_pp = din("gw_pp", [128, 8 * NE])
    gb_bc = din("gb_bc", [128, NE])
    bo_pp = din("bo_pp", [128, 8])
    sb2_pp = din("sb2_pp", [128, 8])
    sb1_pp = din("sb1_pp", [128, 32])
    rb1_pp = din("rb1_pp", [128, NE * 16])
    sw1 = din("sw1", [1024, 4096])
    sw2 = din("sw2", [4096, 1024])
    rw1 = din("rw1", [NE, 1024, 2048])
    rw2 = din("rw2", [NE, 2048, 1024])
    masks = din("masks", [128, 2 * 1024], BF16)
    slotiota = din("slotiota", [128, 512], mybir.dt.int16)
    slotid = din("slotid", [128, 4])
    identb = din("identb", [128, 128], BF16)
    identf = din("identf", [128, 128])
    onehf = din("onehf", [NE, NE * 128])
    out_fm = nc.dram_tensor("out_fm", [1024, OWN], F32, kind="ExternalOutput").ap()
    xn_dram = nc.dram_tensor("xn_tm_scratch", [8, 128, OWN], BF16,
                             kind="Internal").ap()
    cc_in = nc.dram_tensor("cc_in", [HG, 2, 4 * 65, OWN], F32).ap()
    cc_out = nc.dram_tensor("cc_out", [HG, 4 * 65, OWN], F32).ap()

    es_global = ExitStack()
    es_ctx = ExitStack()
    es_xnfm = ExitStack()
    es_s1a = ExitStack()
    es_s1b = ExitStack()
    es_s1c = ExitStack()
    es_acc = ExitStack()
    es_s2a = ExitStack()
    es_s2b = ExitStack()
    es_s2w = ExitStack()
    es_s3 = ExitStack()
    es_sem = ExitStack()
    sems = {e: es_sem.enter_context(nc.semaphore(f"ctr_{e}")) for e in COUNTERS}

    def sb(es, name, shape, dt=F32):
        return es.enter_context(nc.sbuf_tensor(name, shape, dt))

    def ps(es, name, shape, dt=F32):
        return es.enter_context(nc.psum_tensor(name, shape, dt))

    pr = Prog()
    B = {}

    def nb(*names):
        for n in names:
            B[n] = Buf()

    identb_sb = sb(es_global, "identb_sb", [128, 128], BF16)
    ones_sb = sb(es_global, "ones_sb", [1, 128], BF16)

    eps_sb = sb(es_global, "eps_sb", [128, 1])
    bo_sb = sb(es_global, "bo_sb", [128, 8])
    sb2_sb = sb(es_global, "sb2_sb", [128, 8])
    sb1_sb = sb(es_global, "sb1_sb", [128, 32])
    rb1_sb = sb(es_global, "rb1_sb", [128, NE * 16])

    nb("masks", "identb", "identf", "ones", "oneh", "eps", "gw", "gb",
       "rb2", "bo", "sb2", "sb1", "rb1", "gfm", "slotiota", "slotid",
       "gated_tm", "pos_tm", "PM", "xndram")

    pr.dma("sync", identb_sb[:, :], identb, writes=[B["identb"]])
    pr.add("vector", lambda v: v.memset(ones_sb[:, :], 1.0), writes=[B["ones"]])
    pr.dma("sync", bo_sb[:, :], bo_pp, writes=[B["bo"]])
    pr.dma("sync", sb2_sb[:, :], sb2_pp, writes=[B["sb2"]])
    pr.dma("sync", sb1_sb[:, :], sb1_pp, writes=[B["sb1"]])
    pr.dma("sync", rb1_sb[:, :], rb1_pp, writes=[B["rb1"]])
    pr.add("vector", lambda v: v.memset(eps_sb[:, :], 1e-5), writes=[B["eps"]])

    ctx_sb = sb(es_ctx, "ctx_sb", [128, 8 * OWN], BF16)
    nb("ctx")
    if NH < 16:
        pr.add("vector", lambda v: v.memset(ctx_sb[:, :], 0.0),
               writes=[B["ctx"]])

    xnfm_sb = sb(es_xnfm, "xnfm_sb", [128, 8 * T], BF16)
    xnkv_sb = sb(es_xnfm, "xnkv_sb", [128, 8 * OWN], BF16)
    nb("xnfm", "xnkv")

    xt_sb = sb(es_s1a, "xt_sb", [128, 4 * 1024])
    xnt_sb = sb(es_s1a, "xnt_sb", [128, 4 * 1024], BF16)
    stat_sb = sb(es_s1a, "stat_sb", [128, 4 * 12])
    mv_sb = sb(es_s1a, "mv_sb", [128, 4 * 2])
    rstd_sb = sb(es_s1a, "rstd_sb", [128, 4])
    negmr_sb = sb(es_s1a, "negmr_sb", [128, 4])
    psE = [ps(es_s1a, f"psE{i}", [128, 1024], BF16) for i in range(4)]

    nb("xt0", "xt1", "xt2", "xt3", "xnt0", "xnt1", "xnt2", "xnt3",
       "stat0", "stat1", "stat2", "stat3", "mv0", "mv1", "mv2", "mv3",
       "rstd0", "rstd1", "rstd2", "rstd3",
       "negmr0", "negmr1", "negmr2", "negmr3",
       "pE0", "pE1", "pE2", "pE3")

    def emit_ln_a(tc, x_tiled, pfx, xts_t, stat_t, mv_t, rstd_t, negmr_t,
                  depth=4):
        par = tc % depth
        xtB = B[f"{pfx}xt{par}"]
        stB, mvB = B[f"{pfx}stat{par}"], B[f"{pfx}mv{par}"]
        rsB, nmB = B[f"{pfx}rstd{par}"], B[f"{pfx}negmr{par}"]
        xts = xts_t[:, par * 1024:(par + 1) * 1024]
        pr.dma("sync", xts, x_tiled[tc], writes=[xtB])
        st = stat_t[:, par * 12:(par + 1) * 12]
        pr.add("vector", lambda v, xts=xts, st=st: v.bn_stats(st[:, 0:6], xts[:, 0:512]),
               reads=[xtB], writes=[stB])
        pr.add("vector", lambda v, xts=xts, st=st: v.bn_stats(st[:, 6:12], xts[:, 512:1024]),
               reads=[xtB], writes=[stB])
        mv = mv_t[:, par * 2:(par + 1) * 2]
        pr.add("vector", lambda v, st=st, mv=mv: v.bn_aggr(mv, st),
               reads=[stB], writes=[mvB])
        rs = rstd_t[:, par:par + 1]
        pr.add("scalar", lambda s, mv=mv, rs=rs: s.activation(
            rs, mv[:, 1:2], AF.Ln, bias=eps_sb[:, 0:1]),
            reads=[mvB, B["eps"]], writes=[rsB])
        pr.add("scalar", lambda s, rs=rs: s.activation(rs, rs, AF.Exp, scale=-0.5),
               reads=[rsB], writes=[rsB])

    def emit_ln_b(tc, pfx, xts_t, xnts_t, rstd_t, negmr_t, depth=4,
                  mv_t=None, stat_t=None):
        par = tc % depth
        xtB, xntB = B[f"{pfx}xt{par}"], B[f"{pfx}xnt{par}"]
        rsB, nmB = B[f"{pfx}rstd{par}"], B[f"{pfx}negmr{par}"]
        mvB = B[f"{pfx}mv{par}"]
        xts = xts_t[:, par * 1024:(par + 1) * 1024]
        rs = rstd_t[:, par:par + 1]
        nm = negmr_t[:, par:par + 1]
        mv = mv_t[:, par * 2:(par + 1) * 2]
        pr.add("vector", lambda v, mv=mv, rs=rs, nm=nm: v.tensor_scalar(
            nm, mv[:, 0:1], rs, -1.0, op0=ALU.mult, op1=ALU.mult),
            reads=[mvB, rsB], writes=[nmB])
        xnts = xnts_t[:, par * 1024:(par + 1) * 1024]
        pr.add("scalar", lambda s, xts=xts, xnts=xnts, rs=rs, nm=nm: s.activation(
            xnts, xts, AF.Identity, scale=rs, bias=nm),
            reads=[xtB, rsB, nmB], writes=[xntB])
        return xnts, xntB

    x_all_t = x_all.rearrange("(tc p) d -> tc p d", p=128)
    x_kv_t = x_kv.rearrange("(tc p) d -> tc p d", p=128)
    for src_t, n_t, dst_t, dstB, W in ((x_all_t, NTC_ALL, xnfm_sb, B["xnfm"], T),
                                       (x_kv_t, NTC_OWN, xnkv_sb, B["xnkv"], OWN)):
        for it in range(n_t + 2):
            if it >= 2:
                tc = it - 2
                xnts, xntB = emit_ln_b(tc, "", xt_sb, xnt_sb, rstd_sb, negmr_sb,
                                       mv_t=mv_sb)
            if it < n_t:
                emit_ln_a(it, src_t, "", xt_sb, stat_sb, mv_sb, rstd_sb, negmr_sb)
            if it < 2:
                continue
            for j in range(8):
                pb = B[f"pE{j % 4}"]
                pt = psE[j % 4][:, 0:128]
                pr.add("tensor", lambda t, pt=pt, src=xnts[:, j * 128:(j + 1) * 128]:
                       t.transpose(pt, src, identb_sb[:, :]),
                       reads=[xntB, B["identb"]], writes=[pb])
                dst = dst_t[:, j * W + tc * 128: j * W + tc * 128 + 128]
                pr.add("vector", lambda v, pt=pt, dst=dst: v.tensor_copy(dst, pt),
                       reads=[pb], writes=[dstB])

    es_s1a.close()

    pr.new_seg()

    KT = OWN
    NKC = KT // 128

    masks_sb = sb(es_s1b, "masks_sb", [128, 2 * 1024], BF16)
    q_sb = sb(es_s1b, "q_sb", [128, 2 * T], BF16)
    k_sb = sb(es_s1b, "k_sb", [128, 2 * KT], BF16)
    v_sb = sb(es_s1b, "v_sb", [128, NKC * 260], BF16)
    pt_sb = sb(es_s1b, "pt_sb", [128, 5 * 1024], BF16)
    wq_sb = sb(es_s1b, "wq_sb", [128, 8 * 256], BF16)
    wk_sb = sb(es_s1b, "wk_sb", [128, 8 * 256], BF16)
    wv_sb = sb(es_s1b, "wv_sb", [128, 8 * 256], BF16)
    stg_sb = sb(es_s1b, "stg_sb", [128, 2 * 1024])

    psA = ps(es_s1b, "psA", [128, 1024])
    psB = ps(es_s1b, "psB", [128, 1024])
    psG = ps(es_s1b, "psG", [128, 1024])
    psC = ps(es_s1b, "psC", [128, 512])
    psD = ps(es_s1b, "psD", [128, 512])

    pr.dma("sync", masks_sb[:, :], masks, writes=[B["masks"]])

    for k_ in range(HG):
        nb(f"ccin{k_}", f"ccout{k_}")
    nb("q", "k", "v", "wq", "wk", "wv", "stg0", "stg1",
       "pt0", "pt1", "pt2", "pt3", "pt4",
       "pA0", "pA1", "pB0", "pB1", "pG", "pC", "pD")

    wq_r = wq.rearrange("(kc p) o -> p kc o", p=128)
    wk_r = wk.rearrange("(kc p) o -> p kc o", p=128)
    wv_r = wv.rearrange("(kc p) o -> p kc o", p=128)

    it_hp = [0]
    gbase = [0]
    pend_tail = [None]

    def flush_tail():
        if pend_tail[0] is not None:
            fn, pend_tail[0] = pend_tail[0], None
            fn()

    def emit_cc(k):
        pr.add("gpsimd", lambda g, k=k: g.collective_compute(
            "ReduceScatter", ALU.add,
            replica_groups=[[2 * i, 2 * i + 1] for i in range(4)],
            ins=[cc_in[k:k + 1].rearrange("a q r t -> (a q r) t").opt()],
            outs=[cc_out[k:k + 1].rearrange("a r t -> (a r) t").opt()]),
            reads=[B[f"ccin{k}"]], writes=[B[f"ccout{k}"]], cc=True)

    for hg in range(HG):
        flush_tail()
        for wsb, wr, bn in ((wq_sb, wq_r, "wq"), (wk_sb, wk_r, "wk"),
                            (wv_sb, wv_r, "wv")):
            pr.dma("gpsimd",
                   wsb[:, :].rearrange("p (kc o) -> p kc o", kc=8),
                   wr[:, :, hg * 256:(hg + 1) * 256],
                   writes=[B[bn]])
        if hg >= 1:
            emit_cc(hg - 1)
        for dst_sb, dstB, wsb, wB, src_sb, srcB, W, pst, pb0, pb1 in (
                (q_sb, "q", wq_sb, "wq", xnfm_sb, B["xnfm"], T, psA, "pA0", "pA1"),
                (k_sb, "k", wk_sb, "wk", xnkv_sb, B["xnkv"], KT, psB, "pB0", "pB1")):
            for hp in range(2):
                for qt in range(W // 512):
                    i = qt % 2
                    pv = pst[:, i * 512: i * 512 + 512]
                    pvB = B[pb0] if i == 0 else B[pb1]
                    emits = []
                    for kc in range(8):
                        emits.append(
                            lambda t, pv=pv,
                            l=wsb[:, kc * 256 + hp * 128: kc * 256 + hp * 128 + 128],
                            r=src_sb[:, kc * W + qt * 512: kc * W + qt * 512 + 512],
                            s=(kc == 0), e=(kc == 7):
                            t.matmul(pv, l, r, start=s, stop=e))
                    pr.tgroup(emits, reads=[B[wB], srcB], writes=[pvB])
                    dst = dst_sb[:, hp * W + qt * 512: hp * W + qt * 512 + 512]
                    pr.add("vector", lambda v, pv=pv, dst=dst: v.tensor_copy(dst, pv),
                           reads=[pvB], writes=[B[dstB]])
        pr.add("vector", lambda v: v.memset(
            v_sb[:, :].rearrange("p (c h d) -> p c h d", c=NKC, h=4)[:, :, :, 64:65],
            1.0), writes=[B["v"]])
        for tkc in range(NKC):
            i = tkc % 2
            pv = (psC if i == 0 else psD)[:, 0:256]
            pvB = B["pC"] if i == 0 else B["pD"]
            emits = []
            for kc in range(8):
                emits.append(
                    lambda t, pv=pv,
                    l=xnkv_sb[:, kc * KT + tkc * 128: kc * KT + tkc * 128 + 128],
                    r=wv_sb[:, kc * 256: kc * 256 + 256],
                    s=(kc == 0), e=(kc == 7):
                    t.matmul(pv, l, r, start=s, stop=e))
            pr.tgroup(emits, reads=[B["wv"], B["xnkv"]], writes=[pvB])
            dst = v_sb[:, tkc * 260: tkc * 260 + 260].rearrange(
                "p (h d) -> p h d", d=65)[:, :, 0:64]
            src = pv.rearrange("p (h d) -> p h d", d=64)
            pr.add("vector", lambda v, src=src, dst=dst: v.tensor_copy(dst, src),
                   reads=[pvB], writes=[B["v"]])

        for hp in range(2):
            for p in range(NQT):
                par = it_hp[0] % 2
                it_hp[0] += 1
                ncl = 2 * p + 2
                base = gbase[0]
                gbase[0] += ncl
                psS_ring = (psA, psB, psG)
                sB_ring = ([B["pA0"], B["pA1"]], [B["pB0"], B["pB1"]], [B["pG"]])

                def emit_av(ca, cb, ncl=ncl, hp=hp, base=base):
                    for half, avp, avB in ((0, psC, B["pC"]), (1, psD, B["pD"])):
                        emits = []
                        for c in (ca, cb):
                            sl = (base + c) % 5
                            ptv = pt_sb[:, sl * 1024 + half * 512:
                                        sl * 1024 + half * 512 + 512]
                            emits.append(
                                lambda t, avp=avp,
                                l=v_sb[:, c * 260 + (2 * hp + half) * 65:
                                       c * 260 + (2 * hp + half) * 65 + 65],
                                r=ptv, st=(c == 0), e=(c == ncl - 1):
                                t.matmul(avp[0:65, :], l, r, start=st, stop=e))
                        pr.tgroup(emits,
                                  reads=[B["v"], B[f"pt{(base + ca) % 5}"],
                                         B[f"pt{(base + cb) % 5}"]],
                                  writes=[avB])

                for b in range(ncl):
                    psS = psS_ring[b % 3]
                    sB = sB_ring[b % 3]
                    emits = []
                    for half in (0, 1):
                        emits.append(
                            lambda t, pv=psS[:, half * 512:(half + 1) * 512],
                            l=k_sb[64 * half:64 * half + 64,
                                   hp * KT + b * 128: hp * KT + b * 128 + 128],
                            r=q_sb[64 * half:64 * half + 64,
                                   hp * T + p * 512: hp * T + p * 512 + 512]:
                            t.matmul(pv, l, r, start=True, stop=True))
                    pr.tgroup(emits, reads=[B["k"], B["q"]], writes=sB)
                    sl = (base + b) % 5
                    ptB = B[f"pt{sl}"]
                    ptv = pt_sb[:, sl * 1024:sl * 1024 + 1024]
                    pr.add("scalar", lambda sc, pv=psS[:, 0:1024], ptv=ptv:
                           sc.activation(ptv, pv, AF.Exp, scale=0.125),
                           reads=sB, writes=[ptB])
                    if b >= 2 * p:
                        i01 = b - 2 * p
                        pr.add("vector", lambda v, ptv=ptv,
                               m=masks_sb[:, i01 * 1024: i01 * 1024 + 1024]:
                               v.tensor_tensor(ptv, ptv, m, op=ALU.mult),
                               reads=[B["masks"], ptB], writes=[ptB])
                    if b == 1:
                        flush_tail()
                    if b >= 3 and b % 2 == 1:
                        emit_av(b - 3, b - 2)

                def tail(emit_av=emit_av, ncl=ncl, par=par, hg=hg, hp=hp, p=p):
                    emit_av(ncl - 2, ncl - 1)
                    stg = stg_sb[:, par * 1024:(par + 1) * 1024]
                    stgB = B[f"stg{par}"]
                    pr.add("vector", lambda v, stg=stg: v.tensor_copy(
                        stg[0:65, 0:512], psC[0:65, :]),
                        reads=[B["pC"]], writes=[stgB])
                    pr.add("vector", lambda v, stg=stg: v.tensor_copy(
                        stg[0:65, 512:1024], psD[0:65, :]),
                        reads=[B["pD"]], writes=[stgB])
                    h_even = hg * 4 + 2 * hp
                    qh, pl = divmod(p, NQT // 2)
                    for half in (0, 1):
                        h_ = h_even + half
                        hh_, hl_ = divmod(h_, 4)
                        pr.dma("sync",
                               cc_in[hh_, qh, hl_ * 65:(hl_ + 1) * 65,
                                     pl * 512: pl * 512 + 512],
                               stg[0:65, half * 512: half * 512 + 512],
                               reads=[stgB], writes=[B[f"ccin{hh_}"]])
                pend_tail[0] = tail

    flush_tail()
    es_s1b.close()

    es_xnfm.close()

    pr.new_seg()

    acc_sb = sb(es_acc, "acc_sb", [128, 8 * OWN])
    gated_tm_sb = sb(es_acc, "gated_tm", [128, NTC_OWN * NE], BF16)
    pos_tm_sb = sb(es_acc, "pos_tm", [128, NTC_OWN * NE])
    pm_sb = sb(es_acc, "pm_sb", [NE, OWN])
    if 8 * OWN >= 16384:
        pool_sb = ctx_sb
    else:
        pool_sb = sb(es_acc, "pool_sb", [128, 16384], BF16)

    xnofm_sb = sb(es_s2a, "xnofm_sb", [128, 8 * OWN], BF16)

    wo_sb = sb(es_s2b, "wo_sb", [128, 8 * 1024], BF16)
    xt2_sb = sb(es_s2b, "xt2_sb", [128, 2 * 1024])
    xnt2_sb = sb(es_s2b, "xnt2_sb", [128, 2 * 1024])
    xnbf_sb = sb(es_s2b, "xnbf_sb", [128, 2 * 1024], BF16)
    stat2_sb = sb(es_s2b, "stat2_sb", [128, 2 * 12])
    mv2_sb = sb(es_s2b, "mv2_sb", [128, 2 * 2])
    rstd2_sb = sb(es_s2b, "rstd2_sb", [128, 2])
    negmr2_sb = sb(es_s2b, "negmr2_sb", [128, 2])
    gwork_sb = sb(es_s2b, "gwork_sb", [128, 2 * 7 * 16])
    gsc_sb = sb(es_s2b, "gsc_sb", [128, 2 * 16])
    gpad_sb = sb(es_s2b, "gpad_sb", [128, 2 * 128], BF16)
    gfm_sb = sb(es_s2b, "gfm_sb", [NE, OWN], BF16)
    mfm_sb = sb(es_s2b, "mfm_sb", [NE, OWN], BF16)
    identf_sb = sb(es_s2b, "identf_sb", [128, 128])
    rstg_sb = sb(es_s2b, "rstg_sb", [64, OWN])
    lpf_sb = sb(es_s2b, "lpf_sb", [4, OWN // 2])
    lpb_sb = sb(es_s2b, "lpb_sb", [4, OWN], BF16)
    lr_sb = sb(es_s2b, "lr_sb", [128, OWN], BF16)
    onesb_sb = sb(es_s2b, "onesb_sb", [128, 64], BF16)
    gw_sb = sb(es_s2b, "gw_sb", [128, 8 * NE], BF16)
    gb_sb = sb(es_s2b, "gb_sb", [128, NE])
    rb2_sb = sb(es_s2b, "rb2_sb", [NE, 1024], BF16)

    psA2 = ps(es_s2b, "psA2", [128, 1024])
    psB2 = ps(es_s2b, "psB2", [128, 1024])
    psC2 = ps(es_s2b, "psC2", [128, 512])
    psD2 = ps(es_s2b, "psD2", [128, 512])
    psG2 = ps(es_s2b, "psG2", [128, 1024], BF16)
    psH2 = ps(es_s2b, "psH2", [128, 512])

    nb("acc", "xnofm", "wo", "2xt0", "2xt1", "2xt2", "2xt3",
       "2xnt0", "2xnt1", "2xnt2", "2xnt3",
       "xnbf0", "xnbf1", "xnbf2", "xnbf3",
       "2stat0", "2stat1", "2stat2", "2stat3", "2mv0", "2mv1", "2mv2", "2mv3",
       "2rstd0", "2rstd1", "2rstd2", "2rstd3",
       "2negmr0", "2negmr1", "2negmr2", "2negmr3",
       "gwork0", "gwork1", "gsc0", "gsc1", "gpad0", "gpad1", "mfm",
       "rstg", "lpf", "lr", "onesb",
       "lpb", "p2A0", "p2A1", "p2B0", "p2B1", "p2C", "p2D", "p2E", "p2F")

    pr.dma("sync", identf_sb[:, :], identf, writes=[B["identf"]])
    pr.dma("sync", gb_sb[:, :], gb_bc, writes=[B["gb"]])
    pr.dma("gpsimd", gw_sb[:, :], gw_pp, writes=[B["gw"]])

    x_own_fm_r = x_own_fm.rearrange("(oc p) t -> p oc t", p=128)
    acc_r = acc_sb[:, :].rearrange("p (oc t) -> p oc t", oc=8)
    pr.dma("scalar", acc_r[:, 0:4], x_own_fm_r[:, 0:4], writes=[B["acc"]])
    pr.dma("gpsimd", acc_r[:, 4:8], x_own_fm_r[:, 4:8], writes=[B["acc"]])

    wo_r = wo.rearrange("(kc p) o -> p kc o", p=128)
    pr.dma("gpsimd", wo_sb[:, :].rearrange("p (kc o) -> p kc o", kc=8), wo_r,
           writes=[B["wo"]])
    pr.dma("gpsimd", rb2_sb[:, :], rb2, writes=[B["rb2"]])

    emit_cc(HG - 1)

    pr.add("vector", lambda v: v.memset(onesb_sb[:, :], 1.0),
           writes=[B["onesb"]])

    def emit_lq(k):
        rows = cc_out[k:k + 1].rearrange("a (hl r) t -> (a hl) r t", r=65)
        half = OWN // 2
        for ch in range(2):
            pr.dma("sync", lpf_sb[:, :], rows[:, 64, ch * half:(ch + 1) * half],
                   reads=[B[f"ccout{k}"]], writes=[B["lpf"]])
            pr.add("scalar", lambda s: s.activation(lpf_sb[:, :], lpf_sb[:, :],
                   AF.Ln), reads=[B["lpf"]], writes=[B["lpf"]])
            pr.add("scalar", lambda s, ch=ch: s.activation(
                lpb_sb[:, ch * half:(ch + 1) * half], lpf_sb[:, :],
                AF.Exp, scale=-1.0), reads=[B["lpf"]], writes=[B["lpb"]])

    def emit_rhead(h):
            hh, hl = divmod(h, 4)
            pr.dma("sync", rstg_sb[:, :],
                   cc_out[hh, hl * 65:hl * 65 + 64, :],
                   reads=[B[f"ccout{hh}"]], writes=[B["rstg"]])
            pr.dma("sync", lr_sb[64:65, :], lpb_sb[hl:hl + 1, :],
                   reads=[B["lpb"]], writes=[B["lr"]])

    def emit_readback(h, qt):
            pbase = 64 * (h % 2)
            h2 = h // 2
            par = (h * NQO + qt) % 2
            bc = psA2[:, par * 512:(par + 1) * 512]
            bcB = B["p2A0"] if par == 0 else B["p2A1"]
            pr.add("tensor", lambda t, bc=bc, qt=qt: t.matmul(
                bc[0:64, :], onesb_sb[64:65, :],
                lr_sb[64:65, qt * 512: qt * 512 + 512],
                start=True, stop=True), reads=[B["onesb"], B["lr"]],
                writes=[bcB])
            dst = ctx_sb[pbase:pbase + 64, h2 * OWN + qt * 512:
                         h2 * OWN + qt * 512 + 512]
            pr.add("vector", lambda v, dst=dst, qt=qt, bc=bc:
                v.tensor_tensor(dst, rstg_sb[0:64, qt * 512: qt * 512 + 512],
                                bc[0:64, :], op=ALU.mult),
                reads=[B["rstg"], bcB], writes=[B["ctx"]])

    rb_work = [(h, qt) for h in range(NH) for qt in range(NQO)]
    rb_pos = [0]
    lq_done = [False] * HG

    def drain_readback(n):
        while rb_pos[0] < len(rb_work) and n > 0:
            h, qt = rb_work[rb_pos[0]]
            if not lq_done[h // 4]:
                emit_lq(h // 4)
                lq_done[h // 4] = True
            if qt == 0:
                emit_rhead(h)
            emit_readback(h, qt)
            rb_pos[0] += 1
            n -= 1

    x_own_t = x_own.rearrange("(tc p) d -> tc p d", p=128)

    def emit_gate(tc):
        gp = tc % 2
        i = tc % 2
        pg = (psC2 if i == 0 else psD2)[:, 0:NE]
        pgB = B["p2C"] if i == 0 else B["p2D"]
        emits = []
        for kc in range(8):
            emits.append(
                lambda t, pg=pg,
                l=xnofm_sb[:, kc * OWN + tc * 128: kc * OWN + tc * 128 + 128],
                r=gw_sb[:, kc * NE: kc * NE + NE],
                s=(kc == 0), e=(kc == 7):
                t.matmul(pg, l, r, start=s, stop=e))
        pr.tgroup(emits, reads=[B["xnofm"], B["gw"]], writes=[pgB])
        W = gwork_sb[:, gp * 112: gp * 112 + 112]
        gx = W[:, 0:NE]; ge = W[:, 16:16 + NE]; g = W[:, 32:32 + NE]
        g1 = W[:, 48:48 + NE]; g2 = W[:, 64:64 + NE]; tmp = W[:, 80:80 + NE]
        gated = W[:, 96:96 + NE]
        S = gsc_sb[:, gp * 16: gp * 16 + 16]
        mx = S[:, 0:1]; nmx = S[:, 1:2]; sm = S[:, 2:3]; rsm = S[:, 3:4]
        m1 = S[:, 4:5]; m2 = S[:, 5:6]; m3 = S[:, 6:7]
        gwB, gsB = B[f"gwork{gp}"], B[f"gsc{gp}"]
        gpad = gpad_sb[:, gp * 128: gp * 128 + 128]
        gpB = B[f"gpad{gp}"]

        def va(fn, reads, writes):
            pr.add("gpsimd", fn, reads=reads, writes=writes)

        pr.add("vector", lambda v, pg=pg, gx=gx: v.tensor_tensor(
            gx, pg, gb_sb[:, :], op=ALU.add), reads=[pgB, B["gb"]], writes=[gwB])
        pr.add("vector", lambda v, gx=gx, mx=mx: v.reduce_max(mx, gx, axis=AX.X), reads=[gwB], writes=[gsB])
        va(lambda v, mx=mx, nmx=nmx: v.tensor_scalar_mul(nmx, mx, -1.0), [gsB], [gsB])
        pr.add("scalar", lambda s, gx=gx, ge=ge, nmx=nmx: s.activation(
            ge, gx, AF.Exp, bias=nmx), reads=[gwB, gsB], writes=[gwB])
        pr.add("vector", lambda v, ge=ge, sm=sm: v.reduce_sum(sm, ge, axis=AX.X), reads=[gwB], writes=[gsB])
        pr.add("vector", lambda v, sm=sm, rsm=rsm: v.reciprocal(rsm, sm),
               reads=[gsB], writes=[gsB])
        va(lambda v, ge=ge, g=g, rsm=rsm: v.tensor_scalar_mul(g, ge, rsm),
           [gwB, gsB], [gwB])
        pr.add("vector", lambda v, g=g, m1=m1: v.reduce_max(m1, g, axis=AX.X), reads=[gwB], writes=[gsB])
        va(lambda v, g=g, m1=m1, tmp=tmp: v.tensor_scalar(
            tmp, g, m1, None, op0=ALU.is_ge), [gwB, gsB], [gwB])
        va(lambda v, g=g, tmp=tmp: v.tensor_tensor(tmp, tmp, g, op=ALU.mult),
           [gwB], [gwB])
        va(lambda v, g=g, g1=g1, tmp=tmp: v.tensor_tensor(g1, g, tmp, op=ALU.subtract),
           [gwB], [gwB])
        pr.add("vector", lambda v, g1=g1, m2=m2: v.reduce_max(m2, g1, axis=AX.X), reads=[gwB], writes=[gsB])
        va(lambda v, g1=g1, m2=m2, tmp=tmp: v.tensor_scalar(
            tmp, g1, m2, None, op0=ALU.is_ge), [gwB, gsB], [gwB])
        va(lambda v, g1=g1, tmp=tmp: v.tensor_tensor(tmp, tmp, g1, op=ALU.mult),
           [gwB], [gwB])
        va(lambda v, g1=g1, g2=g2, tmp=tmp: v.tensor_tensor(g2, g1, tmp, op=ALU.subtract),
           [gwB], [gwB])
        pr.add("vector", lambda v, g2=g2, m3=m3: v.reduce_max(m3, g2, axis=AX.X), reads=[gwB], writes=[gsB])
        va(lambda v, g=g, m3=m3, tmp=tmp: v.tensor_scalar(
            tmp, g, m3, None, op0=ALU.is_ge), [gwB, gsB], [gwB])
        va(lambda v, g=g, tmp=tmp, gated=gated: v.tensor_tensor(
            gated, g, tmp, op=ALU.mult), [gwB], [gwB])
        va(lambda v, gated=gated, tc=tc: v.tensor_copy(
            gated_tm_sb[:, tc * NE: tc * NE + NE], gated), [gwB], [B["gated_tm"]])
        va(lambda v, gated=gated, gpad=gpad: v.tensor_copy(gpad[:, 0:NE], gated),
           [gwB], [gpB])
        pr.add("tensor", lambda t, gpad=gpad: t.transpose(
            psG2[:, 0:128], gpad, identb_sb[:, :]),
            reads=[gpB, B["identb"]], writes=[B["p2E"]])
        pr.add("vector", lambda v, tc=tc: v.tensor_copy(
            gfm_sb[0:NE, tc * 128:(tc + 1) * 128], psG2[0:NE, 0:128]),
            reads=[B["p2E"]], writes=[B["gfm"]])

    pr.add("vector", lambda v: v.memset(gpad_sb[:, :], 0.0),
           writes=[B["gpad0"], B["gpad1"]])
    for it in range(NTC_OWN + 1):
        if it >= 1:
            tc = it - 1
            par = tc % 2
            xnts, xntB = emit_ln_b(tc, "2", xt2_sb, xnt2_sb, rstd2_sb, negmr2_sb,
                                   depth=2, mv_t=mv2_sb)
        if it < NTC_OWN:
            emit_ln_a(it, x_own_t, "2", xt2_sb, stat2_sb, mv2_sb, rstd2_sb,
                      negmr2_sb, depth=2)
        if it < 1:
            continue
        for j in range(8):
            pb = B["p2F"] if j % 2 == 0 else B["p2D"]
            pt = (psH2 if j % 2 == 0 else psD2)[:, 0:128]
            pr.add("tensor", lambda t, pt=pt, src=xnts[:, j * 128:(j + 1) * 128]:
                   t.transpose(pt, src, identf_sb[:, :]),
                   reads=[xntB, B["identf"]], writes=[pb])
            dst = acc_sb[:, j * OWN + tc * 128: j * OWN + tc * 128 + 128]
            pr.add("vector", lambda v, pt=pt, dst=dst: v.tensor_tensor(
                dst, dst, pt, op=ALU.add), reads=[pb, B["acc"]], writes=[B["acc"]])
            dst2 = xnofm_sb[:, j * OWN + tc * 128: j * OWN + tc * 128 + 128]
            pr.add("vector", lambda v, pt=pt, dst2=dst2: v.tensor_copy(dst2, pt),
                   reads=[pb], writes=[B["xnofm"]])
        xnbf = xnbf_sb[:, par * 1024:(par + 1) * 1024]
        xnbB = B[f"xnbf{par}"]
        pr.add("gpsimd", lambda g, xnts=xnts, xnbf=xnbf: g.tensor_copy(xnbf, xnts),
               reads=[xntB], writes=[xnbB])
        pr.dma("sync", xn_dram[:, :, tc * 128:(tc + 1) * 128].rearrange(
                   "dc p q -> p dc q"),
               xnbf.rearrange("p (dc q) -> p dc q", dc=8),
               reads=[xnbB], writes=[B["xndram"]])
        emit_gate(tc)
        drain_readback(4)

    drain_readback(len(rb_work))

    pr.add("vector", lambda v: v.tensor_tensor(
        bo_sb[:, :], bo_sb[:, :], sb2_sb[:, :], op=ALU.add),
        reads=[B["bo"], B["sb2"]], writes=[B["bo"]])


    pr.add("vector", lambda v: v.tensor_scalar(
        mfm_sb[:, :], gfm_sb[0:NE, :], 0.0, None, op0=ALU.is_gt),
        reads=[B["gfm"]], writes=[B["mfm"]])
    pr.add("vector", lambda v: v.tensor_tensor_scan(
        pm_sb[:, :], mfm_sb[:, :], mfm_sb[:, :], 0.0,
        op0=ALU.add, op1=ALU.bypass), reads=[B["mfm"]], writes=[B["PM"]])
    pr.add("vector", lambda v: v.tensor_tensor(
        pm_sb[:, :], pm_sb[:, :], mfm_sb[:, :], op=ALU.mult),
        reads=[B["PM"], B["mfm"]], writes=[B["PM"]])
    pr.add("vector", lambda v: v.tensor_scalar_add(pm_sb[:, :], pm_sb[:, :], -1.0),
           reads=[B["PM"]], writes=[B["PM"]])
    for tc in range(NTC_OWN):
        pr.add("tensor", lambda t, tc=tc: t.transpose(
            psH2[:, 0:NE], pm_sb[0:NE, tc * 128:(tc + 1) * 128],
            identf_sb[0:NE, 0:NE]),
            reads=[B["PM"], B["identf"]], writes=[B["p2F"]])
        pr.add("vector", lambda v, tc=tc: v.tensor_copy(
            pos_tm_sb[:, tc * NE: tc * NE + NE], psH2[:, 0:NE]),
            reads=[B["p2F"]], writes=[B["pos_tm"]])

    for oc in range(8):
        for qt in range(NTT):
            i = (oc * NTT + qt) % 2
            pv = psA2[:, i * 512: i * 512 + 512]
            pvB = B["p2A0"] if i == 0 else B["p2A1"]
            emits = []
            for kc in range(8):
                emits.append(
                    lambda t, pv=pv,
                    l=wo_sb[:, kc * 1024 + oc * 128: kc * 1024 + oc * 128 + 128],
                    r=ctx_sb[:, kc * OWN + qt * 512: kc * OWN + qt * 512 + 512],
                    s=(kc == 0), e=(kc == 7):
                    t.matmul(pv, l, r, start=s, stop=e))
            pr.tgroup(emits, reads=[B["wo"], B["ctx"]], writes=[pvB])
            dst = acc_sb[:, oc * OWN + qt * 512: oc * OWN + qt * 512 + 512]
            pr.add("vector", lambda v, pv=pv, dst=dst, oc=oc:
                v.scalar_tensor_tensor(
                    dst, pv, bo_sb[:, oc:oc + 1], dst,
                    op0=ALU.add, op1=ALU.add),
                reads=[pvB, B["acc"], B["bo"]], writes=[B["acc"]])

    for oc in range(8):
        for qt in range(NTT):
            i = (oc * NTT + qt) % 2
            pv = psB2[:, i * 512: i * 512 + 512]
            pvB = B["p2B0"] if i == 0 else B["p2B1"]
            pr.add("tensor", lambda t, pv=pv,
                   l=rb2_sb[0:NE, oc * 128:(oc + 1) * 128],
                   r=gfm_sb[0:NE, qt * 512: qt * 512 + 512]:
                   t.matmul(pv, l, r, start=True, stop=True),
                   reads=[B["rb2"], B["gfm"]], writes=[pvB])
            dst = acc_sb[:, oc * OWN + qt * 512: oc * OWN + qt * 512 + 512]
            pr.add("vector", lambda v, pv=pv, dst=dst: v.scalar_tensor_tensor(
                dst, pv, 1.0, dst, op0=ALU.mult, op1=ALU.add),
                reads=[pvB, B["acc"]], writes=[B["acc"]])

    es_s2b.close()

    pr.new_seg()

    w1s_sb = sb(es_s2w, "w1s_sb", [128, 8 * 2048], BF16)
    w2s_sb = sb(es_s2w, "w2s_sb", [128, 16 * 1024], BF16)
    psA4 = ps(es_s2w, "psA4", [128, 1024])
    psB4 = ps(es_s2w, "psB4", [128, 1024])
    nb("w1s", "w2s", "p4A0", "p4A1", "p4B0", "p4B1", "hS")
    h_sb = pool_sb

    sw1_r = sw1.rearrange("(kc p) f -> p kc f", p=128)
    sw2_r = sw2.rearrange("(fc p) o -> p fc o", p=128)

    for half in range(2):
        src1 = sw1_r[:, :, half * 2048:(half + 1) * 2048]
        src2 = sw2_r[:, half * 16:(half + 1) * 16, :]
        pr.dma("gpsimd", w1s_sb[:, :].rearrange("p (kc f) -> p kc f", kc=8), src1,
               writes=[B["w1s"]])
        pr.dma("gpsimd", w2s_sb[:, :].rearrange("p (fc o) -> p fc o", fc=16), src2,
               writes=[B["w2s"]])
        for tt in range(NTT):
            for fc in range(16):
                i = fc % 2
                ph = psA4[:, i * 512: i * 512 + 512]
                phB = B["p4A0"] if i == 0 else B["p4A1"]
                emits = []
                for kc in range(8):
                    emits.append(
                        lambda t, ph=ph,
                        l=w1s_sb[:, kc * 2048 + fc * 128: kc * 2048 + fc * 128 + 128],
                        r=xnofm_sb[:, kc * OWN + tt * 512: kc * OWN + tt * 512 + 512],
                        s=(kc == 0), ee=(kc == 7):
                        t.matmul(ph, l, r, start=s, stop=ee))
                pr.tgroup(emits, reads=[B["w1s"], B["xnofm"]], writes=[phB])
                b1col = sb1_sb[:, half * 16 + fc: half * 16 + fc + 1]
                hdst = h_sb[:, fc * 512:(fc + 1) * 512]
                pr.add("scalar", lambda s, ph=ph, hdst=hdst, b1col=b1col:
                       s.activation(hdst, ph, AF.Gelu, bias=b1col),
                       reads=[phB, B["sb1"]], writes=[B["hS"]])
            for oc in range(8):
                i = oc % 2
                pv = psB4[:, i * 512: i * 512 + 512]
                pvB = B["p4B0"] if i == 0 else B["p4B1"]
                emits = []
                for fc in range(16):
                    emits.append(
                        lambda t, pv=pv,
                        l=w2s_sb[:, fc * 1024 + oc * 128: fc * 1024 + oc * 128 + 128],
                        r=h_sb[:, fc * 512:(fc + 1) * 512],
                        s=(fc == 0), ee=(fc == 15):
                        t.matmul(pv, l, r, start=s, stop=ee))
                pr.tgroup(emits, reads=[B["w2s"], B["hS"]], writes=[pvB])
                dst = acc_sb[:, oc * OWN + tt * 512: oc * OWN + tt * 512 + 512]
                pr.add("vector", lambda v, pv=pv, dst=dst: v.scalar_tensor_tensor(
                    dst, pv, 1.0, dst, op0=ALU.mult, op1=ALU.add),
                    reads=[pvB, B["acc"]], writes=[B["acc"]])

    es_s2w.close()
    es_s2a.close()

    pr.new_seg()

    xn_tm_sb = sb(es_s3, "xn_tm_sb", [128, 2 * (NTC_OWN * 128)], BF16)
    pet_sb = sb(es_s3, "pet_sb", [128, NTC_OWN * 512], BF16)
    pe_sb = sb(es_s3, "pe_sb", [128, SC * 512], BF16)
    gv_sb = sb(es_s3, "gv_sb", [128, 4])
    iota_sb = sb(es_s3, "iota_sb", [128, 512], mybir.dt.int16)
    sid_sb = sb(es_s3, "sid_sb", [128, 4])
    onehf_sb = sb(es_s3, "onehf_sb", [NE, NE * 128])
    w1_sb = sb(es_s3, "w1_sb", [128, 8 * 2048], BF16)
    w2_sb = sb(es_s3, "w2_sb", [128, 16 * 1024], BF16)
    x_sb = pool_sb[:, 8192: 8192 + 8 * 512]
    osm_sb = pool_sb[:, 12288: 12288 + SC * 1024]

    psA3 = ps(es_s3, "psA3", [128, 1024])
    psB3 = ps(es_s3, "psB3", [128, 1024])
    psC3 = ps(es_s3, "psC3", [128, 512])
    psD3 = ps(es_s3, "psD3", [128, 512])

    nb(*[f"pet{i}" for i in range(NTC_OWN)])
    nb("pex")
    nb("xntm0", "xntm1", "pe", "gv", "iota", "sid", "onehf",
       "w1", "w2", "h", "x", "osm",
       "p3A0", "p3A1", "p3B0", "p3B1", "p3C", "p3D")

    pr.dma("sync", iota_sb[:, :], slotiota, writes=[B["iota"]])
    pr.dma("sync", sid_sb[:, :], slotid, writes=[B["sid"]])
    pr.dma("sync", onehf_sb[:, :], onehf, writes=[B["onehf"]])

    rw1_r = rw1.rearrange("e (kc p) f -> e p kc f", p=128)
    rw2_r = rw2.rearrange("e (fc p) o -> e p fc o", p=128)

    DCW = NTC_OWN * 128

    def emit_pet(e):
        for tc in range(NTC_OWN):
            pr.add("vector", lambda v, tc=tc, e=e: v.tensor_scalar(
                pet_sb[:, tc * 512:(tc + 1) * 512], iota_sb[:, :],
                pos_tm_sb[:, tc * NE + e: tc * NE + e + 1], None, op0=ALU.is_equal),
                reads=[B["iota"], B["pos_tm"]], writes=[B[f"pet{tc}"]])

    emit_pet(0)
    for e in range(NE):
        pr.dma("gpsimd", w1_sb[:, :].rearrange("p (kc f) -> p kc f", kc=8),
               rw1_r[e], writes=[B["w1"]])
        pr.dma("gpsimd", w2_sb[:, :].rearrange("p (fc o) -> p fc o", fc=16),
               rw2_r[e], writes=[B["w2"]])
        for sc in range(SC):
            emits = []
            for tc in range(NTC_OWN):
                emits.append(
                    lambda t, sc=sc, tc=tc, e=e:
                    t.matmul(psB3[:, sc:sc + 1],
                             pet_sb[:, tc * 512 + sc * 128: tc * 512 + sc * 128 + 128],
                             gated_tm_sb[:, tc * NE + e: tc * NE + e + 1],
                             start=(tc == 0), stop=(tc == NTC_OWN - 1)))
            pr.tgroup(emits, reads=[B[f"pet{tc}"] for tc in range(NTC_OWN)]
                      + [B["gated_tm"]], writes=[B["p3B0"]])
        pr.add("vector", lambda v: v.tensor_copy(gv_sb[:, 0:SC], psB3[:, 0:SC]),
               reads=[B["p3B0"]], writes=[B["gv"]])
        for dc in range(8):
            sl = dc % 2
            xnsl = xn_tm_sb[:, sl * DCW:(sl + 1) * DCW]
            xnB = B[f"xntm{sl}"]
            pr.dma("sync", xnsl, xn_dram[dc],
                   reads=[B["xndram"]], writes=[xnB])
            i = dc % 2
            pg = (psC3 if i == 0 else psD3)[:, 0:512]
            pgB = B["p3C"] if i == 0 else B["p3D"]
            emits = []
            for tc in range(NTC_OWN):
                emits.append(
                    lambda t, pg=pg, xnsl=xnsl, tc=tc:
                    t.matmul(pg, xnsl[:, tc * 128:(tc + 1) * 128],
                             pet_sb[:, tc * 512:(tc + 1) * 512],
                             start=(tc == 0), stop=(tc == NTC_OWN - 1)))
            pr.tgroup(emits, reads=[xnB] + [B[f"pet{tc}"] for tc in range(NTC_OWN)],
                      writes=[pgB])
            pexdeps = list(B["pex"].rs.values()) + (
                [B["pex"].w] if B["pex"].w is not None else [])
            pr.add("vector", lambda v, pg=pg, dc=dc: v.tensor_copy(
                x_sb[:, dc * 512:(dc + 1) * 512], pg),
                reads=[pgB], writes=[B["x"]], deps=pexdeps)
        if e + 1 < NE:
            emit_pet(e + 1)
        for fc in range(16):
            i = fc % 2
            ph = (psC3 if i == 0 else psD3)[:, 0:512]
            phB = B["p3C"] if i == 0 else B["p3D"]
            emits = []
            for kc in range(8):
                emits.append(
                    lambda t, ph=ph,
                    l=w1_sb[:, kc * 2048 + fc * 128: kc * 2048 + fc * 128 + 128],
                    r=x_sb[:, kc * 512:(kc + 1) * 512],
                    s=(kc == 0), ee=(kc == 7):
                    t.matmul(ph, l, r, start=s, stop=ee))
            pr.tgroup(emits, reads=[B["w1"], B["x"]], writes=[phB])
            b1col = rb1_sb[:, e * 16 + fc: e * 16 + fc + 1]
            hdst = h_sb[:, fc * 512:(fc + 1) * 512]
            pr.add("scalar", lambda s, ph=ph, hdst=hdst, b1col=b1col:
                   s.activation(hdst, ph, AF.Gelu, bias=b1col),
                   reads=[phB, B["rb1"]], writes=[B["h"]])
        for sc in range(SC):
            for dh in range(2):
                i = (sc * 2 + dh) % 2
                pv = (psC3 if i == 0 else psD3)[:, 0:512]
                pvB = B["p3C"] if i == 0 else B["p3D"]
                emits = []
                for fc in range(16):
                    emits.append(
                        lambda t, pv=pv,
                        l=h_sb[:, fc * 512 + sc * 128: fc * 512 + sc * 128 + 128],
                        r=w2_sb[:, fc * 1024 + dh * 512: fc * 1024 + dh * 512 + 512],
                        s=(fc == 0), ee=(fc == 15):
                        t.matmul(pv, l, r, start=s, stop=ee))
                pr.tgroup(emits, reads=[B["h"], B["w2"]], writes=[pvB])
                odst = osm_sb[:, sc * 1024 + dh * 512: sc * 1024 + dh * 512 + 512]
                pr.add("scalar", lambda s, pv=pv, odst=odst, sc=sc: s.activation(
                    odst, pv, AF.Identity, scale=gv_sb[:, sc:sc + 1]),
                    reads=[pvB, B["gv"]], writes=[B["osm"]])
        pe_alt = x_sb
        def build_pe(tt, e=e):
            par_ = tt % 2
            slot = pe_sb if par_ == 0 else pe_alt
            peB = B["pe"] if par_ == 0 else B["pex"]
            pb = psB3[:, 512:1024] if par_ == 0 else psB3[:, 0:512]
            pbB = B["p3B1"] if par_ == 0 else B["p3B0"]
            extra = []
            if par_ == 1:
                xb = B["x"]
                extra = list(xb.rs.values()) + ([xb.w] if xb.w is not None else [])
            pr.add("tensor", lambda t, e=e, tt=tt, pb=pb: t.matmul(
                pb, onehf_sb[0:NE, e * 128:(e + 1) * 128],
                pm_sb[0:NE, tt * 512: tt * 512 + 512], start=True, stop=True),
                reads=[B["onehf"], B["PM"]], writes=[pbB])
            for sc in range(SC):
                pr.add("vector", lambda v, sc=sc, slot=slot, pb=pb: v.tensor_scalar(
                    slot[:, sc * 512:(sc + 1) * 512], pb,
                    sid_sb[:, sc:sc + 1], None, op0=ALU.is_equal),
                    reads=[pbB, B["sid"]], writes=[peB], deps=extra)

        def scatter_tt(tt):
            par_ = tt % 2
            slot = pe_sb if par_ == 0 else pe_alt
            peB = B["pe"] if par_ == 0 else B["pex"]
            for dc in range(8):
                i = dc % 2
                pv = psA3[:, i * 512: i * 512 + 512]
                pvB = B["p3A0"] if i == 0 else B["p3A1"]
                emits = []
                for sc in range(SC):
                    emits.append(
                        lambda t, pv=pv,
                        l=osm_sb[:, sc * 1024 + dc * 128: sc * 1024 + dc * 128 + 128],
                        r=slot[:, sc * 512:(sc + 1) * 512],
                        s=(sc == 0), ee=(sc == SC - 1):
                        t.matmul(pv, l, r, start=s, stop=ee))
                pr.tgroup(emits, reads=[B["osm"], peB], writes=[pvB])
                dst = acc_sb[:, dc * OWN + tt * 512: dc * OWN + tt * 512 + 512]
                pr.add("vector", lambda v, pv=pv, dst=dst: v.scalar_tensor_tensor(
                    dst, pv, 1.0, dst, op0=ALU.mult, op1=ALU.add),
                    reads=[pvB, B["acc"]], writes=[B["acc"]])

        build_pe(0)
        for tt in range(NTT):
            if tt + 1 < NTT:
                build_pe(tt + 1)
            scatter_tt(tt)

    out_r = out_fm.rearrange("(oc p) t -> p oc t", p=128)
    pr.dma("sync", out_r, acc_sb[:, :].rearrange("p (oc t) -> p oc t", oc=8),
           reads=[B["acc"]])

    totals = pr.finalize_and_emit(nc, sems)
    pr.emit_blocks(nc, sems)

    with nc.Block("fin") as block:
        def fin(s):
            s.wait_ge(sems["sync"], totals["sync"])
            s.wait_ge(sems["gpsimd"], totals["gpsimd"])
        block.sync(fin)

    es_s3.close()
    es_acc.close()
    es_ctx.close()
    es_global.close()
    es_sem.close()
    return nc



def host_prep(x, wq, wk, wv, wo, b_o, sw1, sb1, sw2, sb2, rw1, rb1, rw2,
              rb2, gw, gb, cfg):
    T, NH, NE = cfg["T"], cfg["NH"], cfg["NE"]
    OWN = T // 2
    bf = ml_dtypes.bfloat16
    f32 = np.float32

    x = np.ascontiguousarray(np.asarray(x, dtype=f32))
    nseq = x.shape[0]

    def c32(a):
        return np.ascontiguousarray(np.asarray(a, dtype=f32))

    wq, wk, wv, wo = c32(wq), c32(wk), c32(wv), c32(wo)
    sw1, sw2 = c32(sw1).reshape(1024, 4096), c32(sw2).reshape(4096, 1024)
    rw1, rw2 = c32(rw1), c32(rw2)
    rb2a = c32(rb2).reshape(NE, 1024)
    gwa = c32(gw)
    gba = c32(gb).reshape(NE)
    b_o = c32(b_o).reshape(1024)
    sb1a = c32(sb1).reshape(4096)
    sb2a = c32(sb2).reshape(1024)
    rb1a = c32(rb1).reshape(NE, 2048)

    gw_pp = np.ascontiguousarray(
        gwa.reshape(8, 128, NE).transpose(1, 0, 2).reshape(128, 8 * NE))
    gb_bc = np.ascontiguousarray(np.broadcast_to(gba[None, :], (128, NE)))
    bo_pp = np.ascontiguousarray(b_o.reshape(8, 128).T)
    sb2_pp = np.ascontiguousarray(sb2a.reshape(8, 128).T)
    sb1_pp = np.ascontiguousarray(sb1a.reshape(32, 128).T)
    rb1_pp = np.ascontiguousarray(
        rb1a.reshape(NE, 16, 128).transpose(2, 0, 1).reshape(128, NE * 16))

    kk = np.arange(128)[:, None]
    jj = np.arange(512)[None, :]
    identb = np.eye(128, dtype=bf)
    identf = np.eye(128, dtype=f32)
    onehf = np.zeros((NE, NE * 128), dtype=f32)
    for e in range(NE):
        onehf[e, e * 128:(e + 1) * 128] = 1.0
    slotiota = np.tile(np.arange(512, dtype=np.int16), (128, 1))
    slotid = (np.arange(4, dtype=f32)[None, :] * 128
              + np.arange(128, dtype=f32)[:, None])
    slotid = np.ascontiguousarray(slotid)

    shared = dict(wq=wq, wk=wk, wv=wv, wo=wo, rb2=rb2a, gw_pp=gw_pp,
                  gb_bc=gb_bc, bo_pp=bo_pp, sb2_pp=sb2_pp, sb1_pp=sb1_pp,
                  rb1_pp=rb1_pp, sw1=sw1, sw2=sw2, rw1=rw1, rw2=rw2,
                  identb=identb, identf=identf, onehf=onehf,
                  slotiota=slotiota, slotid=slotid)

    in_maps = []
    for c in range(2 * nseq):
        s, r = c // 2, c % 2
        xo = np.ascontiguousarray(x[s, r * OWN:(r + 1) * OWN])
        xkv = np.ascontiguousarray(
            x[s].reshape(T // 128, 128, 1024)[r::2].reshape(OWN, 1024))
        m = np.concatenate(
            [np.tile(np.where(kk + 128 * (2 * i + r) > jj, 0.0, 1.0), (1, 2))
             for i in range(2)], axis=1)
        m_ = dict(shared)
        m_.update(x_all=x[s], x_own=xo, x_kv=xkv,
                  x_own_fm=np.ascontiguousarray(xo.T),
                  masks=m.astype(bf))
        in_maps.append(m_)
    return in_maps


def kernel(L0_attn_x, L0_attn_W_q, L0_attn_W_k, L0_attn_W_v, L0_attn_W_o,
           L0_attn_b_o, L0_ffn_sW1, L0_ffn_sb1, L0_ffn_sW2, L0_ffn_sb2,
           L0_ffn_rW1, L0_ffn_rb1, L0_ffn_rW2, L0_ffn_rb2, L0_ffn_gW,
           L0_ffn_gb):
    cfg = full_cfg()
    in_maps = host_prep(L0_attn_x, L0_attn_W_q, L0_attn_W_k, L0_attn_W_v,
                        L0_attn_W_o, L0_attn_b_o, L0_ffn_sW1, L0_ffn_sb1,
                        L0_ffn_sW2, L0_ffn_sb2, L0_ffn_rW1, L0_ffn_rb1,
                        L0_ffn_rW2, L0_ffn_rb2, L0_ffn_gW, L0_ffn_gb, cfg)
    nc = build_nc(cfg)
    res = run_bass_kernel_spmd(nc, in_maps, core_ids=list(range(8)))
    OWN = cfg["T"] // 2
    out = np.empty((4, cfg["T"], 1024), dtype=np.float32)
    for c in range(8):
        s, r = c // 2, c % 2
        out[s, r * OWN:(r + 1) * OWN] = np.asarray(res.results[c]["out_fm"]).T
    return out
